# version 07
# baseline (speedup 1.0000x reference)
import numpy as np
import ml_dtypes
import concourse.bass as bass
import concourse.mybir as mybir
from concourse.bass_utils import run_bass_kernel_spmd
from contextlib import ExitStack

F32 = mybir.dt.float32
BF16 = mybir.dt.bfloat16
U8 = mybir.dt.uint8
AF = mybir.ActivationFunctionType
ALU = mybir.AluOpType
AX = mybir.AxisListType

D = 2048
KC = 16
H = 16
DH = 128
TOK = 1024
NCH = 8
HG = 8
NE = 16
FI = 1408
NI = 11
FS = 2816
NIS = 22
KB = 1024


class Sem:
    __slots__ = ("h", "n", "name")

    def __init__(self, h, name):
        self.h, self.n, self.name = h, 0, name


class PRes:
    __slots__ = ("ap", "freed")

    def __init__(self, ap):
        self.ap, self.freed = ap, None


PHASES = []


class B:
    def __init__(self):
        self.q = {k: [] for k in ("pe", "dv", "ac", "sp", "gp")}
        self.pe_n = 0

    def mark(self, name):
        PHASES.append((name, self.pe_n))

    def emit(self, eng, meth, *args, inc=None, incn=1, **kw):
        if eng == "pe":
            self.pe_n += 1
        if inc is not None:
            inc.n += incn
        def f(E, meth=meth, args=args, kw=kw, inc=inc, incn=incn):
            ins = getattr(E, meth)(*args, **kw)
            if inc is not None:
                ins.then_inc(inc.h, incn)
        self.q[eng].append(f)

    def emitw(self, eng, meth, *args, **kw):
        sem = self.esem[eng]
        self.emit(eng, meth, *args, inc=sem, **kw)
        self.wait(eng, sem)

    def wait(self, eng, sem, val=None):
        v = sem.n if val is None else val
        if v <= 0:
            return
        self.q[eng].append(lambda E, h=sem.h, v=v: E.wait_ge(h, v))

    def use(self, res):
        if res.freed is not None:
            sem, cnt = res.freed
            self.wait("pe", sem, cnt)
            res.freed = None

    def free(self, res, sem):
        res.freed = (sem, sem.n)


def build(debug=False, stop=None):
    nc = bass.Bass("TRN2", target_bir_lowering=False)
    dr = lambda name, shape, dt=F32, kind="ExternalInput": nc.dram_tensor(name, shape, dt, kind=kind).ap()

    x_own = dr("x_own", [TOK, D])
    x_pre = dr("x_pre", [TOK, D])
    wq = dr("wq", [D, D])
    wk = dr("wk", [D, D])
    wv = dr("wv", [D, D])
    wb_d = dr("wb", [D, H])
    gam_d = dr("gam", [DH, 1])
    wo_d = dr("wo", [D, D])
    gate_w = dr("gate_w", [NE, D])
    w_gate = dr("w_gate", [NE, D, FI])
    w_up = dr("w_up", [NE, D, FI])
    w_down = dr("w_down", [NE, FI, D])
    sw_gate = dr("sw_gate", [D, FS])
    sw_up = dr("sw_up", [D, FS])
    sw_down = dr("sw_down", [FS, D])
    cst = dr("cst", [128, 5 * 128], BF16)
    fconst = dr("fconst", [128, 8], F32)
    out_d = dr("out", [TOK, D], F32, kind="ExternalOutput")
    dbg = {}
    if debug:
        for nm, shape in [("d_xn", [128, KC * TOK]), ("d_q", [128, H * TOK]),
                          ("d_k", [128, H * TOK]), ("d_v", [128, H * TOK]),
                          ("d_S", [128, H * DH]), ("d_o", [128, H * TOK]),
                          ("d_s", [128, NCH * D]), ("d_y", [128, NCH * D]),
                          ("d_beta", [16, TOK]), ("d_comb", [16, TOK]),
                          ("d_Spre", [128, H * DH])]:
            dt = F32 if nm in ("d_S", "d_y", "d_Spre") else BF16
            dbg[nm] = dr(nm, shape, dt, kind="ExternalOutput")

    ctx = ExitStack()
    EN = ctx.enter_context
    with ctx:
        r0 = EN(nc.sbuf_tensor("r0", [128, 32 * KB], U8)).ap()
        r1 = EN(nc.sbuf_tensor("r1", [128, 32 * KB], U8)).ap()
        r2 = EN(nc.sbuf_tensor("r2", [128, 96 * KB], U8)).ap()
        r3 = EN(nc.sbuf_tensor("r3", [128, 32 * KB], U8)).ap()
        r4 = EN(nc.sbuf_tensor("r4", [128, 8 * KB], U8)).ap()
        r5 = EN(nc.sbuf_tensor("r5", [128, 5 * KB], U8)).ap()
        csb = EN(nc.sbuf_tensor("csb", [128, 5 * 128], BF16)).ap()
        gam_sb = EN(nc.sbuf_tensor("gam_sb", [DH, 1], F32)).ap()
        wb_sb = EN(nc.sbuf_tensor("wb_sb", [128, KC * H], BF16)).ap()
        fc = EN(nc.sbuf_tensor("fc", [128, 8], F32)).ap()

        def cv(region, off_b, dt, shape):
            n = int(np.prod(shape))
            nb = n * (2 if dt == BF16 else 4)
            ap = region[:, int(off_b): int(off_b) + nb].bitcast(dt)
            if len(shape) == 2:
                return ap.rearrange("p (a b) -> p a b", a=shape[0])
            return ap

        xn_own = cv(r0, 0, BF16, (KC, TOK))
        xn_pre = cv(r1, 0, BF16, (KC, TOK))
        a_sb = cv(r1, 0, BF16, (NI, TOK))
        combB = cv(r1, 22 * KB, BF16, (TOK,))
        tmp_sb = cv(r1, 24 * KB, BF16, (TOK,))
        t2_sb = cv(r1, 26 * KB, BF16, (TOK,))
        q_sb = cv(r2, 0, BF16, (H, TOK))
        o_sb = q_sb
        k_sb = cv(r2, 32 * KB, BF16, (H, TOK))
        v_sb = cv(r2, 64 * KB, BF16, (H, TOK))
        y_sb = cv(r2, 0, F32, (NCH, D))
        s_sb = cv(r2, 64 * KB, BF16, (NCH, D))
        xbuf = [cv(r2, 32 * KB, F32, (D,)), cv(r2, 40 * KB, F32, (D,))]
        sqscr = cv(r2, 48 * KB, BF16, (D,))
        wsl = [cv(r3, 0, BF16, (KC, 128)), cv(r3, 4 * KB, BF16, (KC, 128))]
        gwld = r2[0:16, 60 * KB:64 * KB].bitcast(BF16)
        l2scrB = cv(r2, 32 * KB, BF16, (TOK,))
        lnscrB = cv(r2, 34 * KB, F32, (TOK,))
        rstdBB = cv(r2, 38 * KB, F32, (TOK,))
        def make_ws(reg, base, preg, pbase, bbreg, bboff, own):
            return dict(
                NT=[cv(reg, base + 0 * KB, BF16, (HG, 128)), cv(reg, base + 2 * KB, BF16, (HG, 128))],
                Nb=[cv(reg, base + 4 * KB, BF16, (HG, 128)), cv(reg, base + 6 * KB, BF16, (HG, 128))],
                kb=cv(reg, base + 8 * KB, BF16, (HG, 128)),
                R=cv(reg, base + 10 * KB, BF16, (HG, 256)),
                kt=cv(preg, pbase + 0 * KB, BF16, (HG, 128)),
                M2T=cv(preg, pbase + 2 * KB, BF16, (HG, 128)),
                bB=cv(bbreg, bboff, BF16, (TOK,)),
                P=(cv(preg, pbase + 4 * KB, BF16, (HG, 128)) if own else None),
            )
        ws_pre = [make_ws(r3, 0, r2, 0, r3, 28 * KB, False),
                  make_ws(r3, 14 * KB, r2, 4 * KB, r3, 30 * KB, False)]
        Sbf_pre = cv(r2, 8 * KB, BF16, (H, 128))
        l2scr = cv(r3, 8 * KB, BF16, (TOK,))
        lnscr = cv(r3, 10 * KB, F32, (TOK,))
        rstdB = cv(r3, 14 * KB, F32, (TOK,))
        lgts = cv(r3, 18 * KB, F32, (NCH, NE))
        top8 = cv(r3, 19 * KB, F32, (NCH, 8))
        escr = cv(r3, int(19.5 * KB), F32, (NE,))
        combt = cv(r3, 20 * KB, BF16, (NCH, NE))
        negm = EN(nc.sbuf_tensor("negm_c", [128, 1], F32)).ap()
        ws_own = [make_ws(r3, 0, r1, 0, r1, 16 * KB, True),
                  make_ws(r3, 14 * KB, r1, 6 * KB, r1, 18 * KB, True)]
        Sbf_own = cv(r1, 12 * KB, BF16, (H, 128))
        wo_sb = cv(r3, 0, BF16, (KC, TOK))
        wgsl = [cv(r3, 0, BF16, (KC, 128)), cv(r3, 4 * KB, BF16, (KC, 128))]
        wusl = [cv(r3, 8 * KB, BF16, (KC, 128)), cv(r3, 12 * KB, BF16, (KC, 128))]
        wdsl = [cv(r3, 16 * KB, BF16, (NI, 256)), cv(r3, int(21.5 * KB), BF16, (NI, 256))]
        xb5 = [cv(r3, 0, F32, (D,)), cv(r3, 8 * KB, F32, (D,))]
        gt5 = cv(r3, 16 * KB, BF16, (D,))
        ob5 = cv(r3, 20 * KB, F32, (D,))
        S_all = cv(r4, 0, F32, (H, DH))
        mean_c = EN(nc.sbuf_tensor("mean_c", [128, 1], F32)).ap()
        ssum_c = EN(nc.sbuf_tensor("ssum_c", [128, 1], F32)).ap()
        var_c = EN(nc.sbuf_tensor("var_c", [128, 1], F32)).ap()
        rstd_c = EN(nc.sbuf_tensor("rstd_c", [128, 1], F32)).ap()
        nmb_c = EN(nc.sbuf_tensor("nmb_c", [128, 1], F32)).ap()
        msq_c = EN(nc.sbuf_tensor("msq_c", [128, 1], F32)).ap()
        lnv_c = EN(nc.sbuf_tensor("lnv_c", [128, 1], F32)).ap()
        beta_own = r5[0:16, 0:2 * KB].bitcast(BF16)
        comb_f = r5[0:16, 2 * KB:4 * KB].bitcast(BF16)
        gwT = r5[:, 4 * KB: 4 * KB + 512].bitcast(BF16)
        beta_pre = r2[0:16, 12 * KB:14 * KB].bitcast(BF16)

        PSA = PRes(EN(nc.psum_tensor("PSA", [128, 1024], F32)).ap())
        PSB = PRes(EN(nc.psum_tensor("PSB", [128, 1024], F32)).ap())
        PSC = PRes(EN(nc.psum_tensor("PSC", [128, 2048], F32)).ap())
        P6 = [PSA, PSB, PSC]
        banks = ([PRes(PSA.ap[:, 0:512]), PRes(PSA.ap[:, 512:1024]),
                  PRes(PSB.ap[:, 0:512]), PRes(PSB.ap[:, 512:1024]),
                  PRes(PSC.ap[:, 0:512]), PRes(PSC.ap[:, 512:1024]),
                  PRes(PSC.ap[:, 1024:1536]), PRes(PSC.ap[:, 1536:2048])])
        bfv = {id(PSA): PSA.ap.bitcast(BF16), id(PSB): PSB.ap.bitcast(BF16),
               id(PSC): PSC.ap.bitcast(BF16)}

        ident = csb[:, 0:128]
        ones = csb[:, 128:256]
        maskLn = csb[:, 256:384]
        maskUn = csb[:, 384:512]
        maskP = csb[:, 512:640]

        sems = []
        def mk(name):
            s = Sem(EN(nc.semaphore(name)), name)
            sems.append(s)
            return s
        sPE, sDV, sAC = mk("sPE"), mk("sDV"), mk("sAC")
        sK = mk("sK")
        sX = [mk("sX0"), mk("sX1")]
        sW = [mk("sW0"), mk("sW1")]
        sGW = [mk("sGW0"), mk("sGW1")]
        sUW = [mk("sUW0"), mk("sUW1")]
        sDWm = [mk("sDW0"), mk("sDW1")]
        sWO = mk("sWO")
        sOUT = mk("sOUT")
        b = B()
        b.esem = {"dv": sDV, "ac": sAC, "pe": sPE}

        def bc(mask_ap):
            return mask_ap.unsqueeze(1).broadcast_to([128, HG, 128])

        def r3d(ap2d):
            return ap2d.rearrange("p (a b) -> p a b", a=HG)

        def barrier():
            for eng in ("pe", "dv", "ac"):
                for s in (sPE, sDV, sAC):
                    b.wait(eng, s)

        def finish():
            barrier()
            b.wait("sp", sPE)
            b.wait("sp", sDV)
            b.wait("sp", sAC)
            b.wait("sp", sOUT)
            with nc.Block() as blk:
                @blk.tensor
                def _(t):
                    for f in b.q["pe"]:
                        f(t)

                @blk.vector
                def _(v):
                    for f in b.q["dv"]:
                        f(v)

                @blk.scalar
                def _(s):
                    for f in b.q["ac"]:
                        f(s)

                @blk.sync
                def _(sy):
                    for f in b.q["sp"]:
                        f(sy)

                @blk.gpsimd
                def _(g):
                    for f in b.q["gp"]:
                        f(g)
            return nc

        b.emit("gp", "dma_start", out=csb, in_=cst, inc=sK, incn=16)
        b.emit("gp", "dma_start", out=gam_sb, in_=gam_d, inc=sK, incn=16)
        b.emit("gp", "dma_start", out=wb_sb.rearrange("p (c h) -> p c h", c=KC),
               in_=wb_d.rearrange("(c p) h -> p c h", p=128), inc=sK, incn=16)
        b.emit("gp", "dma_start", out=gwld, in_=gate_w, inc=sK, incn=16)
        b.emit("gp", "dma_start", out=fc, in_=fconst, inc=sK, incn=16)
        b.wait("dv", sK, 16 * 5)
        b.emit("dv", "memset", S_all, 0.0, inc=sDV)
        b.wait("pe", sK, 16 * 5)
        b.use(PSA)
        for c in range(KC):
            b.emit("pe", "transpose", bfv[id(PSA)][:, c * 16:(c + 1) * 16],
                   gwld[:, c * 128:(c + 1) * 128], ident[0:16, 0:16],
                   inc=(sPE if c == KC - 1 else None))
        b.wait("dv", sPE)
        b.emit("dv", "tensor_copy", gwT, bfv[id(PSA)][:, 0:256], inc=sDV)
        b.free(PSA, sDV)

        if stop == "setup":
            return finish()

        b.mark('setup')
        xfree = [0, 0]

        def p0_tile(xdram, r, dst):
            sl = r % 2
            b.wait("sp", sAC, xfree[sl])
            b.emit("sp", "dma_start", out=xbuf[sl], in_=xdram[r * 128:(r + 1) * 128, :],
                   inc=sX[sl], incn=16)
            b.wait("dv", sX[sl])
            b.emitw("dv", "tensor_reduce", mean_c, xbuf[sl], AX.X, ALU.add)
            b.wait("ac", sX[sl])
            b.wait("ac", sPE)
            b.emit("ac", "activation", sqscr, xbuf[sl], AF.Square, accum_out=ssum_c, inc=sAC)
            b.wait("dv", sAC)
            b.emitw("dv", "tensor_scalar_mul", mean_c, mean_c, 1.0 / D)
            b.emitw("dv", "tensor_tensor", msq_c, mean_c, mean_c, ALU.mult)
            b.emit("dv", "scalar_tensor_tensor", var_c, ssum_c, 1.0 / D, msq_c,
                   ALU.mult, ALU.subtract, inc=sDV)
            b.wait("ac", sDV)
            b.emitw("ac", "activation", lnv_c, var_c, AF.Ln, bias=fc[:, 0:1])
            b.emit("ac", "activation", rstd_c, lnv_c, AF.Exp, scale=fc[:, 1:2], inc=sAC)
            b.wait("dv", sAC)
            b.emit("dv", "scalar_tensor_tensor", nmb_c, mean_c, -1.0, rstd_c,
                   ALU.mult, ALU.mult, inc=sDV)
            b.wait("ac", sDV)
            b.emit("ac", "activation", sqscr, xbuf[sl], AF.Identity, bias=nmb_c,
                   scale=rstd_c, inc=sAC)
            xfree[sl] = sAC.n
            for g2 in range(2):
                pr = P6[g2 % 2]
                b.wait("pe", sAC)
                b.use(pr)
                for cc in range(8):
                    c = g2 * 8 + cc
                    b.emit("pe", "transpose", bfv[id(pr)][:, cc * 128:(cc + 1) * 128],
                           sqscr[:, c * 128:(c + 1) * 128], ident,
                           inc=(sPE if cc == 7 else None))
                b.wait("dv", sPE)
                b.emit("dv", "tensor_copy", dst[:, g2 * 8:(g2 + 1) * 8, r * 128:(r + 1) * 128],
                       bfv[id(pr)][:, 0:1024].rearrange("p (c t) -> p c t", c=8), inc=sDV)
                b.free(pr, sDV)

        b.wait("ac", sK, 16 * 5)
        for r in range(8):
            p0_tile(x_pre, r, xn_pre)
        for r in range(8):
            p0_tile(x_own, r, xn_own)

        if stop == "p0":
            return finish()

        def load_wslice(wdram, m, ring, slot, sem, freecnt):
            src = wdram[:, m * 128:(m + 1) * 128].rearrange("(c p) m -> p c m", p=128)
            b.wait("gp", sPE, freecnt)
            b.emit("gp", "dma_start", out=ring[slot][:, 0:8, :], in_=src[:, 0:8, :],
                   inc=sem[slot], incn=16)
            b.emit("gp", "dma_start", out=ring[slot][:, 8:16, :], in_=src[:, 8:16, :],
                   inc=sem[slot], incn=16)

        def l2norm_inplace(dst_slice, psres, q_scale, scrA, scrB, scrC, width):
            b.wait("ac", sDV)
            b.wait("ac", sPE)
            b.emit("ac", "activation", scrA[:, 0:width], dst_slice, AF.Square, inc=sAC)
            b.use(psres)
            b.wait("pe", sAC)
            nblk = (width + 511) // 512
            for nb2 in range(nblk):
                w2 = min(512, width - nb2 * 512)
                b.emit("pe", "matmul", psres.ap[:, nb2 * 512: nb2 * 512 + w2], ones,
                       scrA[:, nb2 * 512: nb2 * 512 + w2], start=True, stop=True,
                       inc=(sPE if nb2 == nblk - 1 else None))
            b.wait("ac", sPE)
            b.emitw("ac", "activation", scrB[:, 0:width], psres.ap[:, 0:width], AF.Ln,
                    bias=fc[:, 2:3])
            b.free(psres, sAC)
            b.emit("ac", "activation", scrC[:, 0:width], scrB[:, 0:width], AF.Exp,
                   scale=fc[:, 1:2], bias=(fc[:, 3:4] if q_scale else fc[:, 5:6]), inc=sAC)
            b.wait("dv", sAC)
            b.emit("dv", "tensor_tensor", dst_slice, dst_slice, scrC[:, 0:width],
                   ALU.mult, inc=sDV)

        wfree = [0, 0]

        def proj_full(wdram, xn_src, dst, act):
            for m in range(H):
                wslot = m % 2
                load_wslice(wdram, m, wsl, wslot, sW, wfree[wslot])
                b.wait("pe", sW[wslot])
                pr = P6[m % 3]
                b.use(pr)
                for nb in range(2):
                    for c in range(KC):
                        b.emit("pe", "matmul", pr.ap[:, nb * 512:(nb + 1) * 512],
                               wsl[wslot][:, c, :], xn_src[:, c, nb * 512:(nb + 1) * 512],
                               start=(c == 0), stop=(c == KC - 1),
                               inc=(sPE if c == KC - 1 else None))
                wfree[wslot] = sPE.n
                b.wait("ac", sPE)
                if act == "silu":
                    b.emit("ac", "activation", dst[:, m, :], pr.ap[:, 0:1024], AF.Silu, inc=sAC)
                else:
                    b.emit("ac", "copy", dst[:, m, :], pr.ap[:, 0:1024], inc=sAC)
                b.free(pr, sAC)

        def beta_proj(xn_src, beta_dst):
            for nb in range(2):
                b.use(PSA)
                for c in range(KC):
                    b.emit("pe", "matmul", PSA.ap[0:16, 0:512],
                           wb_sb.rearrange("p (c h) -> p c h", c=KC)[:, c, :],
                           xn_src[:, c, nb * 512:(nb + 1) * 512], start=(c == 0),
                           stop=(c == KC - 1), inc=(sPE if c == KC - 1 else None))
                b.wait("ac", sPE)
                b.emit("ac", "activation", beta_dst[:, nb * 512:(nb + 1) * 512],
                       PSA.ap[0:16, 0:512], AF.Sigmoid, inc=sAC)
                b.free(PSA, sAC)

        def rp3(ap2d, w=256):
            return ap2d.rearrange("p (a b) -> p a b", a=HG)

        def pass1_pre_both(wss, k3s, v3s, beta_sb, tcol, q3s=None):
            own = q3s is not None
            for grp, pr in ((0, PSA), (1, PSB)):
                ws = wss[grp]
                g8 = grp * HG
                b.use(pr)
                b.wait("pe", sAC)
                b.wait("pe", sDV)
                for h in range(HG):
                    b.emit("pe", "matmul", pr.ap[:, h * 128:(h + 1) * 128],
                           ident[0:16, g8 + h:g8 + h + 1].broadcast_to([16, 128]),
                           beta_sb[:, tcol:tcol + 128], start=True, stop=True,
                           inc=(sPE if h == HG - 1 else None))
                b.wait("ac", sPE)
                b.emit("ac", "copy", ws["bB"], pr.ap, inc=sAC)
                b.free(pr, sAC)
                ws["bb_ac"] = sAC.n
            b.wait("dv", sPE)
            for grp in range(2):
                ws = wss[grp]
                bB3 = ws["bB"].rearrange("p (a b) -> p a b", a=HG)
                b.wait("dv", sAC, ws["bb_ac"])
                b.emit("dv", "tensor_tensor", ws["kb"], k3s[grp], bB3, ALU.mult, inc=sDV)
                b.emit("dv", "tensor_tensor", ws["kt"], v3s[grp], bB3, ALU.mult, inc=sDV)
                ws["kv_dv"] = sDV.n
            for grp, pr, pr2 in ((0, PSA, PSB), (1, PSB, PSA)):
                ws = wss[grp]
                b.use(pr)
                b.wait("pe", sDV, ws["kv_dv"])
                for h in range(HG):
                    b.emit("pe", "transpose", bfv[id(pr)][:, h * 128:(h + 1) * 128],
                           ws["kt"][:, h, :], ident, inc=(sPE if h == HG - 1 else None))
                b.wait("ac", sPE)
                b.emit("ac", "copy", ws["R"][:, :, 0:128], rp3(bfv[id(pr)][:, 0:1024]), inc=sAC)
                b.free(pr, sAC)
            for grp, pr in ((0, PSA), (1, PSB)):
                ws = wss[grp]
                b.use(pr)
                for h in range(HG):
                    b.emit("pe", "transpose", bfv[id(pr)][:, h * 128:(h + 1) * 128],
                           ws["kb"][:, h, :], ident, inc=(sPE if h == HG - 1 else None))
                b.wait("ac", sPE)
                b.emit("ac", "copy", ws["R"][:, :, 128:256], rp3(bfv[id(pr)][:, 0:1024]), inc=sAC)
                b.free(pr, sAC)
                ws["c_ac"] = sAC.n
            for grp, pr in ((0, PSA), (1, PSB)):
                ws = wss[grp]
                k3 = k3s[grp]
                b.use(pr)
                b.use(PSC)
                for h in range(HG):
                    b.emit("pe", "matmul", pr.ap[:, h * 128:(h + 1) * 128],
                           ws["kb"][:, h, :], k3[:, h, :], start=True, stop=True,
                           inc=(sPE if h == HG - 1 else None))
                for h in range(HG):
                    b.emit("pe", "matmul", PSC.ap[:, h * 128:(h + 1) * 128],
                           k3[:, h, :], ws["kb"][:, h, :], start=True, stop=True,
                           inc=(sPE if h == HG - 1 else None))
                if own:
                    for h in range(HG):
                        b.emit("pe", "matmul", PSC.ap[:, 1024 + h * 128:1024 + (h + 1) * 128],
                               k3[:, h, :], q3s[grp][:, h, :], start=True, stop=True,
                               inc=(sPE if h == HG - 1 else None))
                b.wait("dv", sPE)
                b.emit("dv", "tensor_tensor", ws["Nb"][0], rp3(pr.ap), bc(maskLn), ALU.mult, inc=sDV)
                b.free(pr, sDV)
                b.emit("dv", "tensor_tensor", ws["NT"][0], rp3(PSC.ap[:, 0:1024]), bc(maskUn),
                       ALU.mult, inc=sDV)
                ws["c_ac"] = sAC.n
                if own:
                    b.emit("dv", "tensor_tensor", ws["P"], rp3(PSC.ap[:, 1024:2048]), bc(maskP),
                           ALU.mult, inc=sDV)
                b.free(PSC, sDV)
                ws["c_dv"] = sDV.n

        def chain_step(ws, j):
            cur, nxt = j % 2, 1 - j % 2
            pa = pb = None
            if j < 6:
                if j < 5:
                    b.use(PSA)
                    b.wait("pe", sDV, ws["c_dv"])
                    b.wait("pe", sAC, ws["c_ac"])
                    for h in range(HG):
                        b.emit("pe", "matmul", PSA.ap[:, h * 128:(h + 1) * 128],
                               ws["NT"][cur][:, h, :], ws["Nb"][cur][:, h, :],
                               start=True, stop=True, inc=(sPE if h == HG - 1 else None))
                    pa = sPE.n
                b.use(PSB)
                b.wait("pe", sDV, ws["c_dv"])
                b.wait("pe", sAC, ws["c_ac"])
                for h in range(HG):
                    b.emit("pe", "matmul", PSB.ap[:, h * 128:(h + 1) * 128],
                           ws["Nb"][cur][:, h, :], ws["NT"][cur][:, h, :],
                           start=True, stop=True, inc=(sPE if h == HG - 1 else None))
                pb = sPE.n
            b.use(PSC)
            b.wait("pe", sDV, ws["c_dv"])
            b.wait("pe", sAC, ws["c_ac"])
            for h in range(HG):
                b.emit("pe", "matmul", PSC.ap[:, h * 256:(h + 1) * 256],
                       ws["NT"][cur][:, h, :], ws["R"][:, h, :],
                       start=True, stop=True, inc=(sPE if h == HG - 1 else None))
            pc = sPE.n
            if j < 6:
                if j < 5:
                    b.wait("ac", sPE, pa)
                    b.emit("ac", "copy", ws["Nb"][nxt], rp3(PSA.ap), inc=sAC)
                    b.free(PSA, sAC)
                b.wait("dv", sPE, pb)
                b.emit("dv", "tensor_copy", ws["NT"][nxt], rp3(PSB.ap), inc=sDV)
                b.free(PSB, sDV)
            b.wait("dv", sPE, pc)
            b.emit("dv", "tensor_tensor", ws["R"], ws["R"],
                   PSC.ap.rearrange("p (a b) -> p a b", a=HG), ALU.add, inc=sDV)
            b.free(PSC, sDV)
            ws["c_dv"] = sDV.n
            ws["c_ac"] = sAC.n

        def pass1_post(wss, k3s):
            for grp, pr, pr2 in ((0, PSA, PSB), (1, PSB, PSA)):
                ws = wss[grp]
                b.use(pr)
                b.wait("pe", sDV, ws["c_dv"])
                for h in range(HG):
                    b.emit("pe", "transpose", bfv[id(pr)][:, h * 128:(h + 1) * 128],
                           k3s[grp][:, h, :], ident, inc=(sPE if h == HG - 1 else None))
                b.wait("ac", sPE)
                b.emit("ac", "copy", ws["kt"], rp3(bfv[id(pr)][:, 0:1024]), inc=sAC)
                b.free(pr, sAC)
                ws["kt_ac"] = sAC.n
                b.use(pr2)
                for h in range(HG):
                    b.emit("pe", "transpose", bfv[id(pr2)][:, h * 128:(h + 1) * 128],
                           ws["R"][:, h, 128:256], ident, inc=(sPE if h == HG - 1 else None))
                b.wait("ac", sPE)
                b.emit("ac", "copy", ws["M2T"], rp3(bfv[id(pr2)][:, 0:1024]), inc=sAC)
                b.free(pr2, sAC)
                ws["m2_ac"] = sAC.n

        def pass2(wss, Sbf, q3s=None, o_dsts=None):
            own = q3s is not None
            Sbf3 = Sbf
            b.wait("ac", sDV)
            b.wait("ac", sPE)
            b.emit("ac", "copy", Sbf3, S_all, inc=sAC)
            cast_ac = sAC.n
            for grp, pr in ((0, PSA), (1, PSB)):
                ws = wss[grp]
                b.use(pr)
                b.wait("pe", sAC, cast_ac)
                b.wait("pe", sAC, ws["m2_ac"])
                for h in range(HG):
                    b.emit("pe", "matmul", pr.ap[:, h * 128:(h + 1) * 128],
                           ws["M2T"][:, h, :], Sbf3[:, grp * HG + h, :], start=True, stop=True,
                           inc=(sPE if h == HG - 1 else None))
            b.wait("dv", sPE)
            for grp, pr in ((0, PSA), (1, PSB)):
                ws = wss[grp]
                b.emit("dv", "tensor_tensor", ws["R"][:, :, 0:128], ws["R"][:, :, 0:128],
                       rp3(pr.ap), ALU.subtract, inc=sDV)
                b.free(pr, sDV)
                ws["u_dv"] = sDV.n
            if own:
                b.use(PSC)
                for grp in range(2):
                    ws = wss[grp]
                    b.wait("pe", sDV, ws["u_dv"])
                    for h in range(HG):
                        dst = PSC.ap[:, grp * 1024 + h * 128: grp * 1024 + (h + 1) * 128]
                        b.emit("pe", "matmul", dst, Sbf3[:, grp * HG + h, :],
                               q3s[grp][:, h, :], start=True, stop=False)
                        b.emit("pe", "matmul", dst, ws["R"][:, h, 0:128], ws["P"][:, h, :],
                               start=False, stop=True, inc=(sPE if h == HG - 1 else None))
                b.wait("ac", sPE)
                b.emit("ac", "copy", o_dsts[0], rp3(PSC.ap[:, 0:1024]), inc=sAC)
                b.emit("ac", "copy", o_dsts[1], rp3(PSC.ap[:, 1024:2048]), inc=sAC)
                b.free(PSC, sAC)
            for grp, pr in ((0, PSA), (1, PSB)):
                ws = wss[grp]
                b.use(pr)
                b.wait("pe", sDV, ws["u_dv"])
                b.wait("pe", sAC, ws["kt_ac"])
                for h in range(HG):
                    b.emit("pe", "matmul", pr.ap[:, h * 128:(h + 1) * 128],
                           ws["kt"][:, h, :], ws["R"][:, h, 0:128], start=True, stop=True,
                           inc=(sPE if h == HG - 1 else None))
            b.wait("dv", sPE)
            for grp, pr in ((0, PSA), (1, PSB)):
                b.emit("dv", "tensor_tensor", S_all[:, grp * HG:(grp + 1) * HG, :],
                       S_all[:, grp * HG:(grp + 1) * HG, :], rp3(pr.ap), ALU.add, inc=sDV)
                b.free(pr, sDV)

        def delta_chunk(wss, k3s, v3s, beta_sb, tcol, Sbf, q3s=None, o_dsts=None):
            pass1_pre_both(wss, k3s, v3s, beta_sb, tcol, q3s)
            for j in range(7):
                chain_step(wss[0], j)
                chain_step(wss[1], j)
            pass1_post(wss, k3s)
            pass2(wss, Sbf, q3s, o_dsts)

        b.mark('p0')
        b.wait("pe", sDV)
        beta_proj(xn_pre, beta_pre)
        proj_full(wk, xn_pre, k_sb, None)
        proj_full(wv, xn_pre, v_sb, "silu")
        for m in range(H):
            b.wait("dv", sAC)
            l2norm_inplace(k_sb[:, m, :], P6[m % 3], False, l2scr, lnscr, rstdB, TOK)
        for c in range(NCH):
            k3s = [k_sb[:, grp * HG:(grp + 1) * HG, c * 128:(c + 1) * 128] for grp in range(2)]
            v3s = [v_sb[:, grp * HG:(grp + 1) * HG, c * 128:(c + 1) * 128] for grp in range(2)]
            delta_chunk(ws_pre, k3s, v3s, beta_pre, c * 128, Sbf_pre)

        if debug:
            barrier()
            b.wait("sp", sDV)
            b.emit("sp", "dma_start", out=dbg["d_Spre"],
                   in_=S_all.rearrange("p a b -> p (a b)"), inc=sOUT, incn=16)
        if stop == "p2a":
            return finish()
        barrier()

        b.mark('p1a2a')
        wfree[0] = sPE.n
        wfree[1] = sPE.n
        proj_full(wq, xn_own, q_sb, None)
        proj_full(wk, xn_own, k_sb, None)
        proj_full(wv, xn_own, v_sb, "silu")
        beta_proj(xn_own, beta_own)
        for tt in range(NCH):
            b.use(PSB)
            for c in range(KC):
                b.emit("pe", "matmul", PSB.ap[:, 0:16], xn_own[:, c, tt * 128:(tt + 1) * 128],
                       gwT[:, c * 16:(c + 1) * 16], start=(c == 0), stop=(c == KC - 1),
                       inc=(sPE if c == KC - 1 else None))
            b.wait("ac", sPE)
            b.emit("ac", "copy", lgts[:, tt, :], PSB.ap[:, 0:16], inc=sAC)
            b.free(PSB, sAC)
        for tt in range(NCH):
            b.wait("dv", sAC)
            b.emitw("dv", "max", top8[:, tt, :], lgts[:, tt, :])
            b.emit("dv", "tensor_scalar_mul", negm, top8[:, tt, 0:1], -1.0, inc=sDV)
            b.wait("ac", sDV)
            b.emit("ac", "activation", escr, lgts[:, tt, :], AF.Exp, bias=negm, inc=sAC)
            b.wait("dv", sAC)
            b.emitw("dv", "tensor_tensor", combt[:, tt, :], lgts[:, tt, :],
                    top8[:, tt, 3:4].broadcast_to([128, NE]), ALU.is_ge)
            b.emitw("dv", "tensor_tensor", escr, escr, combt[:, tt, :], ALU.mult)
            b.emitw("dv", "tensor_reduce", negm, escr, AX.X, ALU.add)
            b.emitw("dv", "reciprocal", negm, negm)
            b.emit("dv", "tensor_tensor", combt[:, tt, :], escr,
                   negm.broadcast_to([128, NE]), ALU.mult, inc=sDV)
        b.use(PSA)
        b.wait("pe", sDV)
        for tt in range(NCH):
            b.emit("pe", "transpose", bfv[id(PSA)][0:16, tt * 128:(tt + 1) * 128],
                   combt[:, tt, :], ident, inc=(sPE if tt == NCH - 1 else None))
        b.wait("dv", sPE)
        b.emit("dv", "tensor_copy", comb_f, bfv[id(PSA)][0:16, 0:1024], inc=sDV)
        b.free(PSA, sDV)
        for m in range(H):
            l2norm_inplace(q_sb[:, m, :], P6[m % 3], True, l2scr, lnscr, rstdB, TOK)
        for m in range(H):
            l2norm_inplace(k_sb[:, m, :], P6[m % 3], False, l2scr, lnscr, rstdB, TOK)

        if debug:
            barrier()
            b.wait("sp", sDV)
            b.wait("sp", sAC)
            b.emit("sp", "dma_start", out=dbg["d_xn"], in_=xn_own.rearrange("p a b -> p (a b)"), inc=sOUT, incn=16)
            b.emit("sp", "dma_start", out=dbg["d_q"], in_=q_sb.rearrange("p a b -> p (a b)"), inc=sOUT, incn=16)
            b.emit("sp", "dma_start", out=dbg["d_k"], in_=k_sb.rearrange("p a b -> p (a b)"), inc=sOUT, incn=16)
            b.emit("sp", "dma_start", out=dbg["d_v"], in_=v_sb.rearrange("p a b -> p (a b)"), inc=sOUT, incn=16)
            b.emit("sp", "dma_start", out=dbg["d_beta"], in_=beta_own, inc=sOUT, incn=16)
            b.emit("sp", "dma_start", out=dbg["d_comb"], in_=comb_f, inc=sOUT, incn=16)

        if stop == "p1b":
            return finish()

        b.mark('p1b')
        barrier()
        for c in range(NCH):
            k3s = [k_sb[:, grp * HG:(grp + 1) * HG, c * 128:(c + 1) * 128] for grp in range(2)]
            v3s = [v_sb[:, grp * HG:(grp + 1) * HG, c * 128:(c + 1) * 128] for grp in range(2)]
            q3s = [q_sb[:, grp * HG:(grp + 1) * HG, c * 128:(c + 1) * 128] for grp in range(2)]
            o_dsts = [o_sb[:, grp * HG:(grp + 1) * HG, c * 128:(c + 1) * 128] for grp in range(2)]
            delta_chunk(ws_own, k3s, v3s, beta_own, c * 128, Sbf_own, q3s, o_dsts)

        if debug:
            barrier()
            b.wait("sp", sDV)
            b.wait("sp", sAC)
            b.emit("sp", "dma_start", out=dbg["d_S"], in_=S_all.rearrange("p a b -> p (a b)"), inc=sOUT, incn=16)

        if stop == "p2b":
            return finish()

        b.mark('p2b')
        barrier()
        b.wait("gp", sPE)
        b.wait("gp", sDV)
        b.wait("gp", sAC)
        for cw in range(KC):
            b.emit("gp", "dma_start", out=wo_sb[:, cw, :],
                   in_=wo_d[cw * 128:(cw + 1) * 128, 0:TOK], inc=sWO, incn=16)

        for m in range(H):
            b.wait("ac", sDV)
            b.wait("ac", sPE)
            b.emit("ac", "activation", l2scrB, o_sb[:, m, :], AF.Square, inc=sAC)
            pr = P6[m % 3]
            b.use(pr)
            b.wait("pe", sAC)
            for nb2 in range(2):
                b.emit("pe", "matmul", pr.ap[:, nb2 * 512:(nb2 + 1) * 512], ones,
                       l2scrB[:, nb2 * 512:(nb2 + 1) * 512], start=True, stop=True,
                       inc=(sPE if nb2 == 1 else None))
            b.wait("ac", sPE)
            b.emitw("ac", "activation", lnscrB, pr.ap[:, 0:1024], AF.Ln, scale=fc[:, 4:5], bias=fc[:, 0:1])
            b.free(pr, sAC)
            b.emit("ac", "activation", rstdBB, lnscrB, AF.Exp, scale=fc[:, 1:2], inc=sAC)
            b.wait("dv", sAC)
            b.emit("dv", "scalar_tensor_tensor", o_sb[:, m, :], o_sb[:, m, :],
                   gam_sb[:, 0:1], rstdBB, ALU.mult, ALU.mult, inc=sDV)

        if debug:
            barrier()
            b.wait("sp", sDV)
            b.emit("sp", "dma_start", out=dbg["d_o"], in_=o_sb.rearrange("p a b -> p (a b)"), inc=sOUT, incn=16)

        if stop == "p25":
            return finish()

        b.mark('p25')
        barrier()
        bk = 0
        for half in range(2):
            if half == 1:
                b.wait("gp", sPE)
                for cw in range(KC):
                    b.emit("gp", "dma_start", out=wo_sb[:, cw, :],
                           in_=wo_d[cw * 128:(cw + 1) * 128, TOK:2 * TOK], inc=sWO, incn=16)
            b.wait("pe", sWO, 16 * KC * (half + 1))
            for tt in range(NCH):
                for hb in range(2):
                    pr = banks[bk % 8]
                    bk += 1
                    b.use(pr)
                    for hh2 in range(H):
                        b.emit("pe", "matmul", pr.ap, o_sb[:, hh2, tt * 128:(tt + 1) * 128],
                               wo_sb[:, hh2, hb * 512:(hb + 1) * 512], start=(hh2 == 0),
                               stop=(hh2 == H - 1), inc=(sPE if hh2 == H - 1 else None))
                    b.wait("dv", sPE)
                    b.emit("dv", "tensor_copy",
                           s_sb[:, tt, half * TOK + hb * 512: half * TOK + (hb + 1) * 512],
                           pr.ap, inc=sDV)
                    b.free(pr, sDV)

        if debug:
            barrier()
            b.wait("sp", sDV)
            b.emit("sp", "dma_start", out=dbg["d_s"], in_=s_sb.rearrange("p a b -> p (a b)"), inc=sOUT, incn=16)

        if stop == "p26":
            return finish()

        b.mark('p26')
        barrier()
        gwfree = [sDV.n, sDV.n]
        uwfree = [sDV.n, sDV.n]
        dwfree = [sPE.n, sPE.n]
        tmpfree = [0, 0]

        def load_islice(wdram_2d, i, ring, slot, sem, freecnt, free_eng_sem):
            src = wdram_2d[:, i * 128:(i + 1) * 128].rearrange("(c p) m -> p c m", p=128)
            b.wait("gp", free_eng_sem, freecnt)
            b.emit("gp", "dma_start", out=ring[slot][:, 0:8, :], in_=src[:, 0:8, :],
                   inc=sem[slot], incn=16)
            b.emit("gp", "dma_start", out=ring[slot][:, 8:16, :], in_=src[:, 8:16, :],
                   inc=sem[slot], incn=16)

        def load_dslice(wdram_2d, hq, slot, nchunks, freecnt, row0):
            src = wdram_2d[row0 * 128:(row0 + nchunks) * 128,
                           hq * 256:(hq + 1) * 256].rearrange("(c p) m -> p c m", p=128)
            b.wait("gp", sPE, freecnt)
            b.emit("gp", "dma_start", out=wdsl[slot][:, 0:nchunks, :], in_=src,
                   inc=sDWm[slot], incn=16)

        experts = [("moe", e) for e in range(NE)] + [("sh", 0), ("sh", 1)]
        bk = 0
        for ei, (kind, e) in enumerate(experts):
            if kind == "moe":
                wg2 = w_gate[e]
                wu2 = w_up[e]
                wd2 = w_down[e]
                nic = NI
                i0 = 0
            else:
                wg2 = sw_gate
                wu2 = sw_up
                wd2 = sw_down
                nic = NI
                i0 = e * NI
            if kind == "moe":
                b.wait("pe", sDV)
                for nb in range(2):
                    pr = banks[bk % 8]
                    bk += 1
                    b.use(pr)
                    b.emit("pe", "matmul", pr.ap,
                           ident[0:16, e:e + 1].broadcast_to([16, 128]),
                           comb_f[:, nb * 512:(nb + 1) * 512], start=True, stop=True,
                           inc=sPE)
                    b.wait("ac", sPE)
                    b.emit("ac", "copy", combB[:, nb * 512:(nb + 1) * 512], pr.ap, inc=sAC)
                    b.free(pr, sAC)
            load_islice(wg2, i0 + 0, wgsl, 0, sGW, gwfree[0], sDV)
            load_islice(wu2, i0 + 0, wusl, 0, sUW, uwfree[0], sDV)
            for i in range(nic):
                sl = i % 2
                if i + 1 < nic:
                    load_islice(wg2, i0 + i + 1, wgsl, 1 - sl, sGW, gwfree[1 - sl], sDV)
                    load_islice(wu2, i0 + i + 1, wusl, 1 - sl, sUW, uwfree[1 - sl], sDV)
                b.wait("pe", sGW[sl])
                b.wait("pe", sUW[sl])
                for tb in range(2):
                    prg = banks[bk % 8]; bk += 1
                    pru = banks[bk % 8]; bk += 1
                    b.use(prg)
                    for c in range(KC):
                        b.emit("pe", "matmul", prg.ap, wgsl[sl][:, c, :],
                               xn_own[:, c, tb * 512:(tb + 1) * 512], start=(c == 0),
                               stop=(c == KC - 1), inc=(sPE if c == KC - 1 else None))
                    b.use(pru)
                    for c in range(KC):
                        b.emit("pe", "matmul", pru.ap, wusl[sl][:, c, :],
                               xn_own[:, c, tb * 512:(tb + 1) * 512], start=(c == 0),
                               stop=(c == KC - 1), inc=(sPE if c == KC - 1 else None))
                    b.wait("ac", sPE)
                    b.wait("ac", sDV, tmpfree[tb])
                    b.emit("ac", "activation", tmp_sb[:, tb * 512:(tb + 1) * 512], prg.ap,
                           AF.Silu, inc=sAC)
                    b.free(prg, sAC)
                    b.wait("dv", sAC)
                    if kind == "moe":
                        b.emitw("dv", "tensor_tensor", t2_sb[:, tb * 512:(tb + 1) * 512],
                                pru.ap, combB[:, tb * 512:(tb + 1) * 512], ALU.mult)
                    else:
                        b.emitw("dv", "tensor_copy", t2_sb[:, tb * 512:(tb + 1) * 512],
                                pru.ap)
                    b.free(pru, sDV)
                    b.emit("dv", "tensor_tensor", a_sb[:, i, tb * 512:(tb + 1) * 512],
                           tmp_sb[:, tb * 512:(tb + 1) * 512],
                           t2_sb[:, tb * 512:(tb + 1) * 512], ALU.mult, inc=sDV)
                    tmpfree[tb] = sDV.n
                gwfree[sl] = sDV.n
                uwfree[sl] = sDV.n
            b.wait("pe", sDV)
            load_dslice(wd2, 0, 0, nic, dwfree[0], i0)
            for hq in range(8):
                sl = hq % 2
                if hq + 1 < 8:
                    load_dslice(wd2, hq + 1, 1 - sl, nic, dwfree[1 - sl], i0)
                b.wait("pe", sDWm[sl])
                for tt in range(NCH):
                    pr = banks[bk % 8]; bk += 1
                    b.use(pr)
                    for ic in range(nic):
                        b.emit("pe", "matmul", pr.ap[:, 0:256], a_sb[:, ic, tt * 128:(tt + 1) * 128],
                               wdsl[sl][:, ic, :], start=(ic == 0), stop=(ic == nic - 1),
                               inc=(sPE if ic == nic - 1 else None))
                    b.wait("dv", sPE)
                    if ei == 0:
                        b.emit("dv", "tensor_copy", y_sb[:, tt, hq * 256:(hq + 1) * 256],
                               pr.ap[:, 0:256], inc=sDV)
                    else:
                        b.emit("dv", "tensor_tensor", y_sb[:, tt, hq * 256:(hq + 1) * 256],
                               y_sb[:, tt, hq * 256:(hq + 1) * 256], pr.ap[:, 0:256],
                               ALU.add, inc=sDV)
                    b.free(pr, sDV)
                dwfree[sl] = sPE.n

        if debug:
            barrier()
            b.wait("sp", sDV)
            b.emit("sp", "dma_start", out=dbg["d_y"], in_=y_sb.rearrange("p a b -> p (a b)"), inc=sOUT, incn=16)

        if stop == "p3":
            return finish()

        b.mark('p3')
        barrier()
        x5free = [sDV.n, sDV.n]
        ob5free = 0
        for tt in range(NCH):
            sl = tt % 2
            b.wait("sp", sDV, x5free[sl])
            b.emit("sp", "dma_start", out=xb5[sl], in_=x_own[tt * 128:(tt + 1) * 128, :],
                   inc=sX[sl], incn=16)
            b.wait("ac", sDV)
            b.emit("ac", "activation", gt5, y_sb[:, tt, :], AF.Silu, inc=sAC)
            b.wait("dv", sAC)
            b.wait("dv", sOUT, ob5free)
            b.emitw("dv", "tensor_tensor", ob5, s_sb[:, tt, :], gt5, ALU.mult)
            b.wait("dv", sX[sl])
            b.emit("dv", "tensor_tensor", ob5, ob5, xb5[sl], ALU.add, inc=sDV)
            x5free[sl] = sDV.n
            b.wait("sp", sDV)
            b.emit("sp", "dma_start", out=out_d[tt * 128:(tt + 1) * 128, :], in_=ob5,
                   inc=sOUT, incn=16)
            ob5free = sOUT.n
        b.wait("sp", sOUT)
        b.mark('p5')

        with nc.Block() as blk:
            @blk.tensor
            def _(t):
                for f in b.q["pe"]:
                    f(t)

            @blk.vector
            def _(v):
                for f in b.q["dv"]:
                    f(v)

            @blk.scalar
            def _(s):
                for f in b.q["ac"]:
                    f(s)

            @blk.sync
            def _(sy):
                for f in b.q["sp"]:
                    f(sy)

            @blk.gpsimd
            def _(g):
                for f in b.q["gp"]:
                    f(g)

    return nc


def _consts():
    bf = ml_dtypes.bfloat16
    ident = np.eye(128, dtype=np.float32)
    ones = np.ones((128, 128), dtype=np.float32)
    tri = np.tril(np.ones((128, 128), dtype=np.float32), -1)
    maskLn = -tri
    maskUn = -tri.T
    maskP = (np.arange(128)[None, :] >= np.arange(128)[:, None]).astype(np.float32)
    cst = np.concatenate([ident, ones, maskLn, maskUn, maskP], axis=1).astype(bf)
    return cst


def kernel(**inputs):
    f32 = lambda v: np.ascontiguousarray(np.asarray(v, dtype=np.float32))
    x = f32(inputs["L0_attn_hidden_states"])
    wq = f32(inputs["L0_attn_Wq"]); wk = f32(inputs["L0_attn_Wk"])
    wv = f32(inputs["L0_attn_Wv"]); wb = f32(inputs["L0_attn_Wb"])
    gam = f32(inputs["L0_attn_o_norm_weight"]).reshape(DH, 1)
    wo = f32(inputs["L0_attn_Wo"])
    gate_w = f32(inputs["L0_ffn_gate_w"])
    w_gate = f32(inputs["L0_ffn_w_gate"]); w_up = f32(inputs["L0_ffn_w_up"])
    w_down = f32(inputs["L0_ffn_w_down"])
    sw_gate = f32(inputs["L0_ffn_sw_gate"]); sw_up = f32(inputs["L0_ffn_sw_up"])
    sw_down = f32(inputs["L0_ffn_sw_down"])
    cst = _consts()
    fconst = np.broadcast_to(np.array(
        [1e-5, -0.5, 1e-24, -0.5 * float(np.log(128.0)), 1.0 / 128.0, 0.0, 1.0, 0.0],
        dtype=np.float32), (128, 8)).copy()
    zeros = np.zeros((TOK, D), dtype=np.float32)

    nc = build()
    in_maps = []
    for c in range(8):
        bb, hh = c // 2, c % 2
        in_maps.append(dict(
            x_own=x[bb, hh * TOK:(hh + 1) * TOK],
            x_pre=(zeros if hh == 0 else x[bb, 0:TOK]),
            wq=wq, wk=wk, wv=wv, wb=wb, gam=gam, wo=wo,
            gate_w=gate_w, w_gate=w_gate, w_up=w_up, w_down=w_down,
            sw_gate=sw_gate, sw_up=sw_up, sw_down=sw_down,
            cst=cst, fconst=fconst,
        ))
    res = run_bass_kernel_spmd(nc, in_maps, core_ids=list(range(8)))
    out = np.empty((4, 2048, D), dtype=np.float32)
    for c in range(8):
        bb, hh = c // 2, c % 2
        out[bb, hh * TOK:(hh + 1) * TOK] = res.results[c]["out"]
    return out


# version 08 (final)
# speedup vs baseline: 1.0839x; 1.0839x over previous
import numpy as np
import ml_dtypes
import concourse.bass as bass
import concourse.mybir as mybir
from concourse.bass_utils import run_bass_kernel_spmd
from contextlib import ExitStack

F32 = mybir.dt.float32
BF16 = mybir.dt.bfloat16
U8 = mybir.dt.uint8
AF = mybir.ActivationFunctionType
ALU = mybir.AluOpType
AX = mybir.AxisListType

D = 2048
KC = 16
H = 16
DH = 128
TOK = 1024
NCH = 8
HG = 8
NE = 16
FI = 1408
NI = 11
FS = 2816
NIS = 22
KB = 1024


class Sem:
    __slots__ = ("h", "n", "name")

    def __init__(self, h, name):
        self.h, self.n, self.name = h, 0, name


class PRes:
    __slots__ = ("ap", "freed")

    def __init__(self, ap):
        self.ap, self.freed = ap, None


PHASES = []


class B:
    def __init__(self):
        self.q = {k: [] for k in ("pe", "dv", "ac", "sp", "gp")}
        self.pe_n = 0

    def mark(self, name):
        PHASES.append((name, self.pe_n))

    def emit(self, eng, meth, *args, inc=None, incn=1, **kw):
        if eng == "pe":
            self.pe_n += 1
        if inc is not None:
            inc.n += incn
        def f(E, meth=meth, args=args, kw=kw, inc=inc, incn=incn):
            ins = getattr(E, meth)(*args, **kw)
            if inc is not None:
                ins.then_inc(inc.h, incn)
        self.q[eng].append(f)

    def emitw(self, eng, meth, *args, **kw):
        sem = self.esem[eng]
        self.emit(eng, meth, *args, inc=sem, **kw)
        self.wait(eng, sem)

    def wait(self, eng, sem, val=None):
        v = sem.n if val is None else val
        if v <= 0:
            return
        self.q[eng].append(lambda E, h=sem.h, v=v: E.wait_ge(h, v))

    def use(self, res):
        if res.freed is not None:
            sem, cnt = res.freed
            self.wait("pe", sem, cnt)
            res.freed = None

    def free(self, res, sem):
        res.freed = (sem, sem.n)


def build(debug=False, stop=None):
    nc = bass.Bass("TRN2", target_bir_lowering=False)
    dr = lambda name, shape, dt=F32, kind="ExternalInput": nc.dram_tensor(name, shape, dt, kind=kind).ap()

    x_own = dr("x_own", [TOK, D])
    x_pre = dr("x_pre", [TOK, D])
    wq = dr("wq", [D, D])
    wk = dr("wk", [D, D])
    wv = dr("wv", [D, D])
    wb_d = dr("wb", [D, H])
    gam_d = dr("gam", [DH, 1])
    wo_d = dr("wo", [D, D])
    gate_w = dr("gate_w", [NE, D])
    w_gate = dr("w_gate", [NE, D, FI])
    w_up = dr("w_up", [NE, D, FI])
    w_down = dr("w_down", [NE, FI, D])
    sw_gate = dr("sw_gate", [D, FS])
    sw_up = dr("sw_up", [D, FS])
    sw_down = dr("sw_down", [FS, D])
    cst = dr("cst", [128, 5 * 128], BF16)
    fconst = dr("fconst", [128, 8], F32)
    out_d = dr("out", [TOK, D], F32, kind="ExternalOutput")
    dbg = {}
    if debug:
        for nm, shape in [("d_xn", [128, KC * TOK]), ("d_q", [128, H * TOK]),
                          ("d_k", [128, H * TOK]), ("d_v", [128, H * TOK]),
                          ("d_S", [128, H * DH]), ("d_o", [128, H * TOK]),
                          ("d_s", [128, NCH * D]), ("d_y", [128, NCH * D]),
                          ("d_beta", [16, TOK]), ("d_comb", [16, TOK]),
                          ("d_Spre", [128, H * DH])]:
            dt = F32 if nm in ("d_S", "d_y", "d_Spre") else BF16
            dbg[nm] = dr(nm, shape, dt, kind="ExternalOutput")

    ctx = ExitStack()
    EN = ctx.enter_context
    with ctx:
        r0 = EN(nc.sbuf_tensor("r0", [128, 32 * KB], U8)).ap()
        r1 = EN(nc.sbuf_tensor("r1", [128, 32 * KB], U8)).ap()
        r2 = EN(nc.sbuf_tensor("r2", [128, 96 * KB], U8)).ap()
        r3 = EN(nc.sbuf_tensor("r3", [128, 32 * KB], U8)).ap()
        r4 = EN(nc.sbuf_tensor("r4", [128, 8 * KB], U8)).ap()
        r5 = EN(nc.sbuf_tensor("r5", [128, 5 * KB], U8)).ap()
        csb = EN(nc.sbuf_tensor("csb", [128, 5 * 128], BF16)).ap()
        gam_sb = EN(nc.sbuf_tensor("gam_sb", [DH, 1], F32)).ap()
        wb_sb = EN(nc.sbuf_tensor("wb_sb", [128, KC * H], BF16)).ap()
        fc = EN(nc.sbuf_tensor("fc", [128, 8], F32)).ap()

        def cv(region, off_b, dt, shape):
            n = int(np.prod(shape))
            nb = n * (2 if dt == BF16 else 4)
            ap = region[:, int(off_b): int(off_b) + nb].bitcast(dt)
            if len(shape) == 2:
                return ap.rearrange("p (a b) -> p a b", a=shape[0])
            return ap

        xn_own = cv(r0, 0, BF16, (KC, TOK))
        xn_pre = cv(r1, 0, BF16, (KC, TOK))
        a_sb = cv(r1, 0, BF16, (NI, TOK))
        combB = cv(r1, 22 * KB, BF16, (TOK,))
        combB2 = cv(r1, 28 * KB, BF16, (TOK,))
        tmp_sb = cv(r1, 24 * KB, BF16, (TOK,))
        t2_sb = cv(r1, 26 * KB, BF16, (TOK,))
        q_sb = cv(r2, 0, BF16, (H, TOK))
        o_sb = q_sb
        k_sb = cv(r2, 32 * KB, BF16, (H, TOK))
        v_sb = cv(r2, 64 * KB, BF16, (H, TOK))
        y_sb = cv(r2, 0, F32, (NCH, D))
        s_sb = cv(r2, 64 * KB, BF16, (NCH, D))
        xbuf = [cv(r2, 32 * KB, F32, (D,)), cv(r2, 40 * KB, F32, (D,))]
        sqscr = cv(r2, 48 * KB, BF16, (D,))
        wsl = [cv(r3, 0, BF16, (KC, 128)), cv(r3, 4 * KB, BF16, (KC, 128))]
        gwld = r2[0:16, 60 * KB:64 * KB].bitcast(BF16)
        l2scrB = cv(r2, 32 * KB, BF16, (TOK,))
        lnscrB = cv(r2, 34 * KB, F32, (TOK,))
        rstdBB = cv(r2, 38 * KB, F32, (TOK,))
        def make_ws(reg, base, preg, pbase, bbreg, bboff, own):
            return dict(
                NT=[cv(reg, base + 0 * KB, BF16, (HG, 128)), cv(reg, base + 2 * KB, BF16, (HG, 128))],
                Nb=[cv(reg, base + 4 * KB, BF16, (HG, 128)), cv(reg, base + 6 * KB, BF16, (HG, 128))],
                kb=cv(reg, base + 8 * KB, BF16, (HG, 128)),
                R=cv(reg, base + 10 * KB, BF16, (HG, 256)),
                kt=cv(preg, pbase + 0 * KB, BF16, (HG, 128)),
                M2T=cv(preg, pbase + 2 * KB, BF16, (HG, 128)),
                bB=cv(bbreg, bboff, BF16, (TOK,)),
                P=(cv(preg, pbase + 4 * KB, BF16, (HG, 128)) if own else None),
            )
        ws_pre = [make_ws(r3, 0, r2, 0, r3, 28 * KB, False),
                  make_ws(r3, 14 * KB, r2, 4 * KB, r3, 30 * KB, False)]
        Sbf_pre = cv(r2, 8 * KB, BF16, (H, 128))
        l2scr = cv(r3, 8 * KB, BF16, (TOK,))
        lnscr = cv(r3, 10 * KB, F32, (TOK,))
        rstdB = cv(r3, 14 * KB, F32, (TOK,))
        lgts = cv(r3, 18 * KB, F32, (NCH, NE))
        top8 = cv(r3, 19 * KB, F32, (NCH, 8))
        escr = cv(r3, int(19.5 * KB), F32, (NE,))
        combt = cv(r3, 20 * KB, BF16, (NCH, NE))
        negm = EN(nc.sbuf_tensor("negm_c", [128, 1], F32)).ap()
        ws_own = [make_ws(r3, 0, r1, 0, r1, 16 * KB, True),
                  make_ws(r3, 14 * KB, r1, 6 * KB, r1, 18 * KB, True)]
        Sbf_own = cv(r1, 12 * KB, BF16, (H, 128))
        wo_sb = cv(r3, 0, BF16, (KC, TOK))
        wgsl = [cv(r3, 0, BF16, (KC, 128)), cv(r3, 4 * KB, BF16, (KC, 128))]
        wusl = [cv(r3, 8 * KB, BF16, (KC, 128)), cv(r3, 12 * KB, BF16, (KC, 128))]
        wdsl = [cv(r3, 16 * KB, BF16, (NI, 256)), cv(r3, int(21.5 * KB), BF16, (NI, 256))]
        xb5 = [cv(r3, 0, F32, (D,)), cv(r3, 8 * KB, F32, (D,))]
        gt5 = cv(r3, 16 * KB, BF16, (D,))
        ob5 = cv(r3, 20 * KB, F32, (D,))
        S_all = cv(r4, 0, F32, (H, DH))
        mean_c = EN(nc.sbuf_tensor("mean_c", [128, 1], F32)).ap()
        ssum_c = EN(nc.sbuf_tensor("ssum_c", [128, 1], F32)).ap()
        var_c = EN(nc.sbuf_tensor("var_c", [128, 1], F32)).ap()
        rstd_c = EN(nc.sbuf_tensor("rstd_c", [128, 1], F32)).ap()
        nmb_c = EN(nc.sbuf_tensor("nmb_c", [128, 1], F32)).ap()
        msq_c = EN(nc.sbuf_tensor("msq_c", [128, 1], F32)).ap()
        lnv_c = EN(nc.sbuf_tensor("lnv_c", [128, 1], F32)).ap()
        beta_own = r5[0:16, 0:2 * KB].bitcast(BF16)
        comb_f = r5[0:16, 2 * KB:4 * KB].bitcast(BF16)
        gwT = r5[:, 4 * KB: 4 * KB + 512].bitcast(BF16)
        beta_pre = r2[0:16, 12 * KB:14 * KB].bitcast(BF16)

        PSA = PRes(EN(nc.psum_tensor("PSA", [128, 1024], F32)).ap())
        PSB = PRes(EN(nc.psum_tensor("PSB", [128, 1024], F32)).ap())
        PSC = PRes(EN(nc.psum_tensor("PSC", [128, 2048], F32)).ap())
        P6 = [PSA, PSB, PSC]
        banks = ([PRes(PSA.ap[:, 0:512]), PRes(PSA.ap[:, 512:1024]),
                  PRes(PSB.ap[:, 0:512]), PRes(PSB.ap[:, 512:1024]),
                  PRes(PSC.ap[:, 0:512]), PRes(PSC.ap[:, 512:1024]),
                  PRes(PSC.ap[:, 1024:1536]), PRes(PSC.ap[:, 1536:2048])])
        bfv = {id(PSA): PSA.ap.bitcast(BF16), id(PSB): PSB.ap.bitcast(BF16),
               id(PSC): PSC.ap.bitcast(BF16)}

        ident = csb[:, 0:128]
        ones = csb[:, 128:256]
        maskLn = csb[:, 256:384]
        maskUn = csb[:, 384:512]
        maskP = csb[:, 512:640]

        sems = []
        def mk(name):
            s = Sem(EN(nc.semaphore(name)), name)
            sems.append(s)
            return s
        sPE, sDV, sAC = mk("sPE"), mk("sDV"), mk("sAC")
        sK = mk("sK")
        sX = [mk("sX0"), mk("sX1")]
        sW = [mk("sW0"), mk("sW1")]
        sGW = [mk("sGW0"), mk("sGW1")]
        sUW = [mk("sUW0"), mk("sUW1")]
        sDWm = [mk("sDW0"), mk("sDW1")]
        sWO = mk("sWO")
        sOUT = mk("sOUT")
        b = B()
        b.esem = {"dv": sDV, "ac": sAC, "pe": sPE}

        def bc(mask_ap):
            return mask_ap.unsqueeze(1).broadcast_to([128, HG, 128])

        def r3d(ap2d):
            return ap2d.rearrange("p (a b) -> p a b", a=HG)

        def barrier():
            for eng in ("pe", "dv", "ac"):
                for s in (sPE, sDV, sAC):
                    b.wait(eng, s)

        def finish():
            barrier()
            b.wait("sp", sPE)
            b.wait("sp", sDV)
            b.wait("sp", sAC)
            b.wait("sp", sOUT)
            with nc.Block() as blk:
                @blk.tensor
                def _(t):
                    for f in b.q["pe"]:
                        f(t)

                @blk.vector
                def _(v):
                    for f in b.q["dv"]:
                        f(v)

                @blk.scalar
                def _(s):
                    for f in b.q["ac"]:
                        f(s)

                @blk.sync
                def _(sy):
                    for f in b.q["sp"]:
                        f(sy)

                @blk.gpsimd
                def _(g):
                    for f in b.q["gp"]:
                        f(g)
            return nc

        b.emit("gp", "dma_start", out=csb, in_=cst, inc=sK, incn=16)
        b.emit("gp", "dma_start", out=gam_sb, in_=gam_d, inc=sK, incn=16)
        b.emit("gp", "dma_start", out=wb_sb.rearrange("p (c h) -> p c h", c=KC),
               in_=wb_d.rearrange("(c p) h -> p c h", p=128), inc=sK, incn=16)
        b.emit("gp", "dma_start", out=gwld, in_=gate_w, inc=sK, incn=16)
        b.emit("gp", "dma_start", out=fc, in_=fconst, inc=sK, incn=16)
        b.wait("dv", sK, 16 * 5)
        b.emit("dv", "memset", S_all, 0.0, inc=sDV)
        b.wait("pe", sK, 16 * 5)
        b.use(PSA)
        for c in range(KC):
            b.emit("pe", "transpose", bfv[id(PSA)][:, c * 16:(c + 1) * 16],
                   gwld[:, c * 128:(c + 1) * 128], ident[0:16, 0:16],
                   inc=(sPE if c == KC - 1 else None))
        b.wait("dv", sPE)
        b.emit("dv", "tensor_copy", gwT, bfv[id(PSA)][:, 0:256], inc=sDV)
        b.free(PSA, sDV)

        if stop == "setup":
            return finish()

        b.mark('setup')
        xfree = [0, 0]

        def p0_tile(xdram, r, dst):
            sl = r % 2
            b.wait("sp", sAC, xfree[sl])
            b.emit("sp", "dma_start", out=xbuf[sl], in_=xdram[r * 128:(r + 1) * 128, :],
                   inc=sX[sl], incn=16)
            b.wait("dv", sX[sl])
            b.emitw("dv", "tensor_reduce", mean_c, xbuf[sl], AX.X, ALU.add)
            b.wait("ac", sX[sl])
            b.wait("ac", sPE)
            b.emit("ac", "activation", sqscr, xbuf[sl], AF.Square, accum_out=ssum_c, inc=sAC)
            b.wait("dv", sAC)
            b.emitw("dv", "tensor_scalar_mul", mean_c, mean_c, 1.0 / D)
            b.emitw("dv", "tensor_tensor", msq_c, mean_c, mean_c, ALU.mult)
            b.emit("dv", "scalar_tensor_tensor", var_c, ssum_c, 1.0 / D, msq_c,
                   ALU.mult, ALU.subtract, inc=sDV)
            b.wait("ac", sDV)
            b.emitw("ac", "activation", lnv_c, var_c, AF.Ln, bias=fc[:, 0:1])
            b.emit("ac", "activation", rstd_c, lnv_c, AF.Exp, scale=fc[:, 1:2], inc=sAC)
            b.wait("dv", sAC)
            b.emit("dv", "scalar_tensor_tensor", nmb_c, mean_c, -1.0, rstd_c,
                   ALU.mult, ALU.mult, inc=sDV)
            b.wait("ac", sDV)
            b.emit("ac", "activation", sqscr, xbuf[sl], AF.Identity, bias=nmb_c,
                   scale=rstd_c, inc=sAC)
            xfree[sl] = sAC.n
            for g2 in range(2):
                pr = P6[g2 % 2]
                b.wait("pe", sAC)
                b.use(pr)
                for cc in range(8):
                    c = g2 * 8 + cc
                    b.emit("pe", "transpose", bfv[id(pr)][:, cc * 128:(cc + 1) * 128],
                           sqscr[:, c * 128:(c + 1) * 128], ident,
                           inc=(sPE if cc == 7 else None))
                b.wait("dv", sPE)
                b.emit("dv", "tensor_copy", dst[:, g2 * 8:(g2 + 1) * 8, r * 128:(r + 1) * 128],
                       bfv[id(pr)][:, 0:1024].rearrange("p (c t) -> p c t", c=8), inc=sDV)
                b.free(pr, sDV)

        b.wait("ac", sK, 16 * 5)
        for r in range(8):
            p0_tile(x_pre, r, xn_pre)
        for r in range(8):
            p0_tile(x_own, r, xn_own)

        if stop == "p0":
            return finish()

        def load_wslice(wdram, m, ring, slot, sem, freecnt):
            src = wdram[:, m * 128:(m + 1) * 128].rearrange("(c p) m -> p c m", p=128)
            b.wait("gp", sPE, freecnt)
            b.emit("gp", "dma_start", out=ring[slot][:, 0:8, :], in_=src[:, 0:8, :],
                   inc=sem[slot], incn=16)
            b.emit("gp", "dma_start", out=ring[slot][:, 8:16, :], in_=src[:, 8:16, :],
                   inc=sem[slot], incn=16)

        def l2norm_inplace(dst_slice, psres, q_scale, scrA, scrB, scrC, width):
            b.wait("ac", sDV)
            b.wait("ac", sPE)
            b.emit("ac", "activation", scrA[:, 0:width], dst_slice, AF.Square, inc=sAC)
            b.use(psres)
            b.wait("pe", sAC)
            nblk = (width + 511) // 512
            for nb2 in range(nblk):
                w2 = min(512, width - nb2 * 512)
                b.emit("pe", "matmul", psres.ap[:, nb2 * 512: nb2 * 512 + w2], ones,
                       scrA[:, nb2 * 512: nb2 * 512 + w2], start=True, stop=True,
                       inc=(sPE if nb2 == nblk - 1 else None))
            b.wait("ac", sPE)
            b.emitw("ac", "activation", scrB[:, 0:width], psres.ap[:, 0:width], AF.Ln,
                    bias=fc[:, 2:3])
            b.free(psres, sAC)
            b.emit("ac", "activation", scrC[:, 0:width], scrB[:, 0:width], AF.Exp,
                   scale=fc[:, 1:2], bias=(fc[:, 3:4] if q_scale else fc[:, 5:6]), inc=sAC)
            b.wait("dv", sAC)
            b.emit("dv", "tensor_tensor", dst_slice, dst_slice, scrC[:, 0:width],
                   ALU.mult, inc=sDV)

        wfree = [0, 0]

        def proj_slice(wdram, m, xn_src, dst, act, pr):
            wslot = m % 2
            load_wslice(wdram, m, wsl, wslot, sW, wfree[wslot])
            b.wait("pe", sW[wslot])
            b.use(pr)
            for nb in range(2):
                for c in range(KC):
                    b.emit("pe", "matmul", pr.ap[:, nb * 512:(nb + 1) * 512],
                           wsl[wslot][:, c, :], xn_src[:, c, nb * 512:(nb + 1) * 512],
                           start=(c == 0), stop=(c == KC - 1),
                           inc=(sPE if c == KC - 1 else None))
            wfree[wslot] = sPE.n
            b.wait("ac", sPE)
            if act == "silu":
                b.emit("ac", "activation", dst[:, m, :], pr.ap[:, 0:1024], AF.Silu, inc=sAC)
            else:
                b.emit("ac", "copy", dst[:, m, :], pr.ap[:, 0:1024], inc=sAC)
            b.free(pr, sAC)

        def proj_full(wdram, xn_src, dst, act):
            for m in range(H):
                proj_slice(wdram, m, xn_src, dst, act, P6[m % 3])

        def beta_proj(xn_src, beta_dst):
            for nb in range(2):
                b.use(PSA)
                for c in range(KC):
                    b.emit("pe", "matmul", PSA.ap[0:16, 0:512],
                           wb_sb.rearrange("p (c h) -> p c h", c=KC)[:, c, :],
                           xn_src[:, c, nb * 512:(nb + 1) * 512], start=(c == 0),
                           stop=(c == KC - 1), inc=(sPE if c == KC - 1 else None))
                b.wait("ac", sPE)
                b.emit("ac", "activation", beta_dst[:, nb * 512:(nb + 1) * 512],
                       PSA.ap[0:16, 0:512], AF.Sigmoid, inc=sAC)
                b.free(PSA, sAC)

        def rp3(ap2d, w=256):
            return ap2d.rearrange("p (a b) -> p a b", a=HG)

        def pass1_pre_both(wss, k3s, v3s, beta_sb, tcol, q3s=None):
            own = q3s is not None
            for grp, pr in ((0, PSA), (1, PSB)):
                ws = wss[grp]
                g8 = grp * HG
                b.use(pr)
                b.wait("pe", sAC)
                b.wait("pe", sDV)
                for h in range(HG):
                    b.emit("pe", "matmul", pr.ap[:, h * 128:(h + 1) * 128],
                           ident[0:16, g8 + h:g8 + h + 1].broadcast_to([16, 128]),
                           beta_sb[:, tcol:tcol + 128], start=True, stop=True,
                           inc=(sPE if h == HG - 1 else None))
                b.wait("ac", sPE)
                b.emit("ac", "copy", ws["bB"], pr.ap, inc=sAC)
                b.free(pr, sAC)
                ws["bb_ac"] = sAC.n
            b.wait("dv", sPE)
            for grp in range(2):
                ws = wss[grp]
                bB3 = ws["bB"].rearrange("p (a b) -> p a b", a=HG)
                b.wait("dv", sAC, ws["bb_ac"])
                b.emit("dv", "tensor_tensor", ws["kb"], k3s[grp], bB3, ALU.mult, inc=sDV)
                b.emit("dv", "tensor_tensor", ws["kt"], v3s[grp], bB3, ALU.mult, inc=sDV)
                ws["kv_dv"] = sDV.n
            for grp, pr, pr2 in ((0, PSA, PSB), (1, PSB, PSA)):
                ws = wss[grp]
                b.use(pr)
                b.wait("pe", sDV, ws["kv_dv"])
                for h in range(HG):
                    b.emit("pe", "transpose", bfv[id(pr)][:, h * 128:(h + 1) * 128],
                           ws["kt"][:, h, :], ident, inc=(sPE if h == HG - 1 else None))
                b.wait("ac", sPE)
                b.emit("ac", "copy", ws["R"][:, :, 0:128], rp3(bfv[id(pr)][:, 0:1024]), inc=sAC)
                b.free(pr, sAC)
            for grp, pr in ((0, PSA), (1, PSB)):
                ws = wss[grp]
                b.use(pr)
                for h in range(HG):
                    b.emit("pe", "transpose", bfv[id(pr)][:, h * 128:(h + 1) * 128],
                           ws["kb"][:, h, :], ident, inc=(sPE if h == HG - 1 else None))
                b.wait("ac", sPE)
                b.emit("ac", "copy", ws["R"][:, :, 128:256], rp3(bfv[id(pr)][:, 0:1024]), inc=sAC)
                b.free(pr, sAC)
                ws["c_ac"] = sAC.n
            for grp, pr in ((0, PSA), (1, PSB)):
                ws = wss[grp]
                k3 = k3s[grp]
                b.use(pr)
                b.use(PSC)
                for h in range(HG):
                    b.emit("pe", "matmul", pr.ap[:, h * 128:(h + 1) * 128],
                           ws["kb"][:, h, :], k3[:, h, :], start=True, stop=True,
                           inc=(sPE if h == HG - 1 else None))
                for h in range(HG):
                    b.emit("pe", "matmul", PSC.ap[:, h * 128:(h + 1) * 128],
                           k3[:, h, :], ws["kb"][:, h, :], start=True, stop=True,
                           inc=(sPE if h == HG - 1 else None))
                if own:
                    for h in range(HG):
                        b.emit("pe", "matmul", PSC.ap[:, 1024 + h * 128:1024 + (h + 1) * 128],
                               k3[:, h, :], q3s[grp][:, h, :], start=True, stop=True,
                               inc=(sPE if h == HG - 1 else None))
                b.wait("dv", sPE)
                b.emit("dv", "tensor_tensor", ws["Nb"][0], rp3(pr.ap), bc(maskLn), ALU.mult, inc=sDV)
                b.free(pr, sDV)
                b.emit("dv", "tensor_tensor", ws["NT"][0], rp3(PSC.ap[:, 0:1024]), bc(maskUn),
                       ALU.mult, inc=sDV)
                ws["c_ac"] = sAC.n
                if own:
                    b.emit("dv", "tensor_tensor", ws["P"], rp3(PSC.ap[:, 1024:2048]), bc(maskP),
                           ALU.mult, inc=sDV)
                b.free(PSC, sDV)
                ws["c_dv"] = sDV.n

        def chain_step(ws, j):
            cur, nxt = j % 2, 1 - j % 2
            pa = pb = None
            if j < 6:
                if j < 5:
                    b.use(PSA)
                    b.wait("pe", sDV, ws["c_dv"])
                    b.wait("pe", sAC, ws["c_ac"])
                    for h in range(HG):
                        b.emit("pe", "matmul", PSA.ap[:, h * 128:(h + 1) * 128],
                               ws["NT"][cur][:, h, :], ws["Nb"][cur][:, h, :],
                               start=True, stop=True, inc=(sPE if h == HG - 1 else None))
                    pa = sPE.n
                b.use(PSB)
                b.wait("pe", sDV, ws["c_dv"])
                b.wait("pe", sAC, ws["c_ac"])
                for h in range(HG):
                    b.emit("pe", "matmul", PSB.ap[:, h * 128:(h + 1) * 128],
                           ws["Nb"][cur][:, h, :], ws["NT"][cur][:, h, :],
                           start=True, stop=True, inc=(sPE if h == HG - 1 else None))
                pb = sPE.n
            b.use(PSC)
            b.wait("pe", sDV, ws["c_dv"])
            b.wait("pe", sAC, ws["c_ac"])
            for h in range(HG):
                b.emit("pe", "matmul", PSC.ap[:, h * 256:(h + 1) * 256],
                       ws["NT"][cur][:, h, :], ws["R"][:, h, :],
                       start=True, stop=True, inc=(sPE if h == HG - 1 else None))
            pc = sPE.n
            if j < 6:
                if j < 5:
                    b.wait("ac", sPE, pa)
                    b.emit("ac", "copy", ws["Nb"][nxt], rp3(PSA.ap), inc=sAC)
                    b.free(PSA, sAC)
                b.wait("dv", sPE, pb)
                b.emit("dv", "tensor_copy", ws["NT"][nxt], rp3(PSB.ap), inc=sDV)
                b.free(PSB, sDV)
            b.wait("dv", sPE, pc)
            b.emit("dv", "tensor_tensor", ws["R"], ws["R"],
                   PSC.ap.rearrange("p (a b) -> p a b", a=HG), ALU.add, inc=sDV)
            b.free(PSC, sDV)
            ws["c_dv"] = sDV.n
            ws["c_ac"] = sAC.n

        def pass1_post(wss, k3s):
            for grp, pr, pr2 in ((0, PSA, PSB), (1, PSB, PSA)):
                ws = wss[grp]
                b.use(pr)
                b.wait("pe", sDV, ws["c_dv"])
                for h in range(HG):
                    b.emit("pe", "transpose", bfv[id(pr)][:, h * 128:(h + 1) * 128],
                           k3s[grp][:, h, :], ident, inc=(sPE if h == HG - 1 else None))
                b.wait("ac", sPE)
                b.emit("ac", "copy", ws["kt"], rp3(bfv[id(pr)][:, 0:1024]), inc=sAC)
                b.free(pr, sAC)
                ws["kt_ac"] = sAC.n
                b.use(pr2)
                for h in range(HG):
                    b.emit("pe", "transpose", bfv[id(pr2)][:, h * 128:(h + 1) * 128],
                           ws["R"][:, h, 128:256], ident, inc=(sPE if h == HG - 1 else None))
                b.wait("ac", sPE)
                b.emit("ac", "copy", ws["M2T"], rp3(bfv[id(pr2)][:, 0:1024]), inc=sAC)
                b.free(pr2, sAC)
                ws["m2_ac"] = sAC.n

        def pass2(wss, Sbf, q3s=None, o_dsts=None):
            own = q3s is not None
            Sbf3 = Sbf
            b.wait("ac", sDV)
            b.wait("ac", sPE)
            b.emit("ac", "copy", Sbf3, S_all, inc=sAC)
            cast_ac = sAC.n
            for grp, pr in ((0, PSA), (1, PSB)):
                ws = wss[grp]
                b.use(pr)
                b.wait("pe", sAC, cast_ac)
                b.wait("pe", sAC, ws["m2_ac"])
                for h in range(HG):
                    b.emit("pe", "matmul", pr.ap[:, h * 128:(h + 1) * 128],
                           ws["M2T"][:, h, :], Sbf3[:, grp * HG + h, :], start=True, stop=True,
                           inc=(sPE if h == HG - 1 else None))
            b.wait("dv", sPE)
            for grp, pr in ((0, PSA), (1, PSB)):
                ws = wss[grp]
                b.emit("dv", "tensor_tensor", ws["R"][:, :, 0:128], ws["R"][:, :, 0:128],
                       rp3(pr.ap), ALU.subtract, inc=sDV)
                b.free(pr, sDV)
                ws["u_dv"] = sDV.n
            if own:
                b.use(PSC)
                for grp in range(2):
                    ws = wss[grp]
                    b.wait("pe", sDV, ws["u_dv"])
                    for h in range(HG):
                        dst = PSC.ap[:, grp * 1024 + h * 128: grp * 1024 + (h + 1) * 128]
                        b.emit("pe", "matmul", dst, Sbf3[:, grp * HG + h, :],
                               q3s[grp][:, h, :], start=True, stop=False)
                        b.emit("pe", "matmul", dst, ws["R"][:, h, 0:128], ws["P"][:, h, :],
                               start=False, stop=True, inc=(sPE if h == HG - 1 else None))
                b.wait("ac", sPE)
                b.emit("ac", "copy", o_dsts[0], rp3(PSC.ap[:, 0:1024]), inc=sAC)
                b.emit("ac", "copy", o_dsts[1], rp3(PSC.ap[:, 1024:2048]), inc=sAC)
                b.free(PSC, sAC)
            for grp, pr in ((0, PSA), (1, PSB)):
                ws = wss[grp]
                b.use(pr)
                b.wait("pe", sDV, ws["u_dv"])
                b.wait("pe", sAC, ws["kt_ac"])
                for h in range(HG):
                    b.emit("pe", "matmul", pr.ap[:, h * 128:(h + 1) * 128],
                           ws["kt"][:, h, :], ws["R"][:, h, 0:128], start=True, stop=True,
                           inc=(sPE if h == HG - 1 else None))
            b.wait("dv", sPE)
            for grp, pr in ((0, PSA), (1, PSB)):
                b.emit("dv", "tensor_tensor", S_all[:, grp * HG:(grp + 1) * HG, :],
                       S_all[:, grp * HG:(grp + 1) * HG, :], rp3(pr.ap), ALU.add, inc=sDV)
                b.free(pr, sDV)

        def delta_chunk(wss, k3s, v3s, beta_sb, tcol, Sbf, q3s=None, o_dsts=None):
            pass1_pre_both(wss, k3s, v3s, beta_sb, tcol, q3s)
            for j in range(7):
                chain_step(wss[0], j)
                chain_step(wss[1], j)
            pass1_post(wss, k3s)
            pass2(wss, Sbf, q3s, o_dsts)

        b.mark('p0')
        b.wait("pe", sDV)
        beta_proj(xn_pre, beta_pre)
        proj_full(wk, xn_pre, k_sb, None)
        proj_full(wv, xn_pre, v_sb, "silu")
        for m in range(H):
            b.wait("dv", sAC)
            l2norm_inplace(k_sb[:, m, :], P6[m % 3], False, l2scr, lnscr, rstdB, TOK)
        for c in range(NCH):
            k3s = [k_sb[:, grp * HG:(grp + 1) * HG, c * 128:(c + 1) * 128] for grp in range(2)]
            v3s = [v_sb[:, grp * HG:(grp + 1) * HG, c * 128:(c + 1) * 128] for grp in range(2)]
            delta_chunk(ws_pre, k3s, v3s, beta_pre, c * 128, Sbf_pre)

        if debug:
            barrier()
            b.wait("sp", sDV)
            b.emit("sp", "dma_start", out=dbg["d_Spre"],
                   in_=S_all.rearrange("p a b -> p (a b)"), inc=sOUT, incn=16)
        if stop == "p2a":
            return finish()
        barrier()

        b.mark('p1a2a')
        wfree[0] = sPE.n
        wfree[1] = sPE.n
        proj_full(wq, xn_own, q_sb, None)
        proj_full(wk, xn_own, k_sb, None)
        for m in range(H):
            proj_slice(wv, m, xn_own, v_sb, "silu", P6[m % 3])
            l2norm_inplace(q_sb[:, m, :], P6[(m + 2) % 3], True, l2scr, lnscr, rstdB, TOK)
        beta_proj(xn_own, beta_own)
        P6b = [PSA, PSC]
        for tt in range(NCH):
            b.use(PSB)
            for c in range(KC):
                b.emit("pe", "matmul", PSB.ap[:, 0:16], xn_own[:, c, tt * 128:(tt + 1) * 128],
                       gwT[:, c * 16:(c + 1) * 16], start=(c == 0), stop=(c == KC - 1),
                       inc=(sPE if c == KC - 1 else None))
            b.wait("ac", sPE)
            b.emit("ac", "copy", lgts[:, tt, :], PSB.ap[:, 0:16], inc=sAC)
            b.free(PSB, sAC)
            b.wait("dv", sAC)
            b.emitw("dv", "max", top8[:, tt, :], lgts[:, tt, :])
            b.emit("dv", "tensor_scalar_mul", negm, top8[:, tt, 0:1], -1.0, inc=sDV)
            b.wait("ac", sDV)
            b.emit("ac", "activation", escr, lgts[:, tt, :], AF.Exp, bias=negm, inc=sAC)
            b.wait("dv", sAC)
            b.emitw("dv", "tensor_tensor", combt[:, tt, :], lgts[:, tt, :],
                    top8[:, tt, 3:4].broadcast_to([128, NE]), ALU.is_ge)
            b.emitw("dv", "tensor_tensor", escr, escr, combt[:, tt, :], ALU.mult)
            b.emitw("dv", "tensor_reduce", negm, escr, AX.X, ALU.add)
            b.emitw("dv", "reciprocal", negm, negm)
            b.emit("dv", "tensor_tensor", combt[:, tt, :], escr,
                   negm.broadcast_to([128, NE]), ALU.mult, inc=sDV)
            l2norm_inplace(k_sb[:, 2 * tt, :], P6b[0], False, l2scr, lnscr, rstdB, TOK)
            l2norm_inplace(k_sb[:, 2 * tt + 1, :], P6b[1], False, l2scr, lnscr, rstdB, TOK)
        b.use(PSA)
        b.wait("pe", sDV)
        for tt in range(NCH):
            b.emit("pe", "transpose", bfv[id(PSA)][0:16, tt * 128:(tt + 1) * 128],
                   combt[:, tt, :], ident, inc=(sPE if tt == NCH - 1 else None))
        b.wait("dv", sPE)
        b.emit("dv", "tensor_copy", comb_f, bfv[id(PSA)][0:16, 0:1024], inc=sDV)
        b.free(PSA, sDV)

        if debug:
            barrier()
            b.wait("sp", sDV)
            b.wait("sp", sAC)
            b.emit("sp", "dma_start", out=dbg["d_xn"], in_=xn_own.rearrange("p a b -> p (a b)"), inc=sOUT, incn=16)
            b.emit("sp", "dma_start", out=dbg["d_q"], in_=q_sb.rearrange("p a b -> p (a b)"), inc=sOUT, incn=16)
            b.emit("sp", "dma_start", out=dbg["d_k"], in_=k_sb.rearrange("p a b -> p (a b)"), inc=sOUT, incn=16)
            b.emit("sp", "dma_start", out=dbg["d_v"], in_=v_sb.rearrange("p a b -> p (a b)"), inc=sOUT, incn=16)
            b.emit("sp", "dma_start", out=dbg["d_beta"], in_=beta_own, inc=sOUT, incn=16)
            b.emit("sp", "dma_start", out=dbg["d_comb"], in_=comb_f, inc=sOUT, incn=16)

        if stop == "p1b":
            return finish()

        b.mark('p1b')
        barrier()
        for c in range(NCH):
            k3s = [k_sb[:, grp * HG:(grp + 1) * HG, c * 128:(c + 1) * 128] for grp in range(2)]
            v3s = [v_sb[:, grp * HG:(grp + 1) * HG, c * 128:(c + 1) * 128] for grp in range(2)]
            q3s = [q_sb[:, grp * HG:(grp + 1) * HG, c * 128:(c + 1) * 128] for grp in range(2)]
            o_dsts = [o_sb[:, grp * HG:(grp + 1) * HG, c * 128:(c + 1) * 128] for grp in range(2)]
            delta_chunk(ws_own, k3s, v3s, beta_own, c * 128, Sbf_own, q3s, o_dsts)

        if debug:
            barrier()
            b.wait("sp", sDV)
            b.wait("sp", sAC)
            b.emit("sp", "dma_start", out=dbg["d_S"], in_=S_all.rearrange("p a b -> p (a b)"), inc=sOUT, incn=16)

        if stop == "p2b":
            return finish()

        b.mark('p2b')
        barrier()
        b.wait("gp", sPE)
        b.wait("gp", sDV)
        b.wait("gp", sAC)
        for cw in range(KC):
            b.emit("gp", "dma_start", out=wo_sb[:, cw, :],
                   in_=wo_d[cw * 128:(cw + 1) * 128, 0:TOK], inc=sWO, incn=16)

        for m in range(H):
            b.wait("ac", sDV)
            b.wait("ac", sPE)
            b.emit("ac", "activation", l2scrB, o_sb[:, m, :], AF.Square, inc=sAC)
            pr = P6[m % 3]
            b.use(pr)
            b.wait("pe", sAC)
            for nb2 in range(2):
                b.emit("pe", "matmul", pr.ap[:, nb2 * 512:(nb2 + 1) * 512], ones,
                       l2scrB[:, nb2 * 512:(nb2 + 1) * 512], start=True, stop=True,
                       inc=(sPE if nb2 == 1 else None))
            b.wait("ac", sPE)
            b.emitw("ac", "activation", lnscrB, pr.ap[:, 0:1024], AF.Ln, scale=fc[:, 4:5], bias=fc[:, 0:1])
            b.free(pr, sAC)
            b.emit("ac", "activation", rstdBB, lnscrB, AF.Exp, scale=fc[:, 1:2], inc=sAC)
            b.wait("dv", sAC)
            b.emit("dv", "scalar_tensor_tensor", o_sb[:, m, :], o_sb[:, m, :],
                   gam_sb[:, 0:1], rstdBB, ALU.mult, ALU.mult, inc=sDV)

        if debug:
            barrier()
            b.wait("sp", sDV)
            b.emit("sp", "dma_start", out=dbg["d_o"], in_=o_sb.rearrange("p a b -> p (a b)"), inc=sOUT, incn=16)

        if stop == "p25":
            return finish()

        b.mark('p25')
        barrier()
        bk = 0
        for half in range(2):
            if half == 1:
                b.wait("gp", sPE)
                for cw in range(KC):
                    b.emit("gp", "dma_start", out=wo_sb[:, cw, :],
                           in_=wo_d[cw * 128:(cw + 1) * 128, TOK:2 * TOK], inc=sWO, incn=16)
            b.wait("pe", sWO, 16 * KC * (half + 1))
            for tt in range(NCH):
                for hb in range(2):
                    pr = banks[bk % 8]
                    bk += 1
                    b.use(pr)
                    for hh2 in range(H):
                        b.emit("pe", "matmul", pr.ap, o_sb[:, hh2, tt * 128:(tt + 1) * 128],
                               wo_sb[:, hh2, hb * 512:(hb + 1) * 512], start=(hh2 == 0),
                               stop=(hh2 == H - 1), inc=(sPE if hh2 == H - 1 else None))
                    b.wait("dv", sPE)
                    b.emit("dv", "tensor_copy",
                           s_sb[:, tt, half * TOK + hb * 512: half * TOK + (hb + 1) * 512],
                           pr.ap, inc=sDV)
                    b.free(pr, sDV)

        if debug:
            barrier()
            b.wait("sp", sDV)
            b.emit("sp", "dma_start", out=dbg["d_s"], in_=s_sb.rearrange("p a b -> p (a b)"), inc=sOUT, incn=16)

        if stop == "p26":
            return finish()

        b.mark('p26')
        barrier()
        gwfree = [sDV.n, sDV.n]
        uwfree = [sDV.n, sDV.n]
        dwfree = [sPE.n, sPE.n]
        tmpfree = [0, 0]

        def load_islice(wdram_2d, i, ring, slot, sem, freecnt, free_eng_sem):
            src = wdram_2d[:, i * 128:(i + 1) * 128].rearrange("(c p) m -> p c m", p=128)
            b.wait("gp", free_eng_sem, freecnt)
            b.emit("gp", "dma_start", out=ring[slot][:, 0:8, :], in_=src[:, 0:8, :],
                   inc=sem[slot], incn=16)
            b.emit("gp", "dma_start", out=ring[slot][:, 8:16, :], in_=src[:, 8:16, :],
                   inc=sem[slot], incn=16)

        def load_dslice(wdram_2d, hq, slot, nchunks, freecnt, row0):
            src = wdram_2d[row0 * 128:(row0 + nchunks) * 128,
                           hq * 256:(hq + 1) * 256].rearrange("(c p) m -> p c m", p=128)
            b.wait("gp", sPE, freecnt)
            b.emit("gp", "dma_start", out=wdsl[slot][:, 0:nchunks, :], in_=src,
                   inc=sDWm[slot], incn=16)

        experts = [("moe", e) for e in range(NE)] + [("sh", 0), ("sh", 1)]
        bk = 0
        combBs = [combB, combB2]
        combfree = [0, 0]

        def comb_bcast(e):
            nonlocal bk
            buf = e % 2
            b.wait("pe", sDV, combfree[buf])
            for nb in range(2):
                pr = banks[bk % 8]
                bk += 1
                b.use(pr)
                b.emit("pe", "matmul", pr.ap,
                       ident[0:16, e:e + 1].broadcast_to([16, 128]),
                       comb_f[:, nb * 512:(nb + 1) * 512], start=True, stop=True,
                       inc=sPE)
                b.wait("ac", sPE)
                b.emit("ac", "copy", combBs[buf][:, nb * 512:(nb + 1) * 512], pr.ap, inc=sAC)
                b.free(pr, sAC)

        comb_bcast(0)
        for ei, (kind, e) in enumerate(experts):
            if kind == "moe":
                wg2 = w_gate[e]
                wu2 = w_up[e]
                wd2 = w_down[e]
                nic = NI
                i0 = 0
            else:
                wg2 = sw_gate
                wu2 = sw_up
                wd2 = sw_down
                nic = NI
                i0 = e * NI
            load_islice(wg2, i0 + 0, wgsl, 0, sGW, gwfree[0], sDV)
            load_islice(wu2, i0 + 0, wusl, 0, sUW, uwfree[0], sDV)
            for i in range(nic):
                sl = i % 2
                if i + 1 < nic:
                    load_islice(wg2, i0 + i + 1, wgsl, 1 - sl, sGW, gwfree[1 - sl], sDV)
                    load_islice(wu2, i0 + i + 1, wusl, 1 - sl, sUW, uwfree[1 - sl], sDV)
                b.wait("pe", sGW[sl])
                b.wait("pe", sUW[sl])
                for tb in range(2):
                    prg = banks[bk % 8]; bk += 1
                    pru = banks[bk % 8]; bk += 1
                    b.use(prg)
                    for c in range(KC):
                        b.emit("pe", "matmul", prg.ap, wgsl[sl][:, c, :],
                               xn_own[:, c, tb * 512:(tb + 1) * 512], start=(c == 0),
                               stop=(c == KC - 1), inc=(sPE if c == KC - 1 else None))
                    b.use(pru)
                    for c in range(KC):
                        b.emit("pe", "matmul", pru.ap, wusl[sl][:, c, :],
                               xn_own[:, c, tb * 512:(tb + 1) * 512], start=(c == 0),
                               stop=(c == KC - 1), inc=(sPE if c == KC - 1 else None))
                    b.wait("ac", sPE)
                    b.wait("ac", sDV, tmpfree[tb])
                    b.emit("ac", "activation", tmp_sb[:, tb * 512:(tb + 1) * 512], prg.ap,
                           AF.Silu, inc=sAC)
                    b.free(prg, sAC)
                    b.wait("dv", sAC)
                    if kind == "moe":
                        b.emitw("dv", "tensor_tensor", t2_sb[:, tb * 512:(tb + 1) * 512],
                                pru.ap, combBs[e % 2][:, tb * 512:(tb + 1) * 512], ALU.mult)
                    else:
                        b.emitw("dv", "tensor_copy", t2_sb[:, tb * 512:(tb + 1) * 512],
                                pru.ap)
                    b.free(pru, sDV)
                    b.emit("dv", "tensor_tensor", a_sb[:, i, tb * 512:(tb + 1) * 512],
                           tmp_sb[:, tb * 512:(tb + 1) * 512],
                           t2_sb[:, tb * 512:(tb + 1) * 512], ALU.mult, inc=sDV)
                    tmpfree[tb] = sDV.n
                gwfree[sl] = sDV.n
                uwfree[sl] = sDV.n
            if kind == "moe":
                combfree[e % 2] = sDV.n
                if e + 1 < NE:
                    comb_bcast(e + 1)
            b.wait("pe", sDV)
            load_dslice(wd2, 0, 0, nic, dwfree[0], i0)
            for hq in range(8):
                sl = hq % 2
                if hq + 1 < 8:
                    load_dslice(wd2, hq + 1, 1 - sl, nic, dwfree[1 - sl], i0)
                b.wait("pe", sDWm[sl])
                for tt in range(NCH):
                    pr = banks[bk % 8]; bk += 1
                    b.use(pr)
                    for ic in range(nic):
                        b.emit("pe", "matmul", pr.ap[:, 0:256], a_sb[:, ic, tt * 128:(tt + 1) * 128],
                               wdsl[sl][:, ic, :], start=(ic == 0), stop=(ic == nic - 1),
                               inc=(sPE if ic == nic - 1 else None))
                    b.wait("dv", sPE)
                    if ei == 0:
                        b.emit("dv", "tensor_copy", y_sb[:, tt, hq * 256:(hq + 1) * 256],
                               pr.ap[:, 0:256], inc=sDV)
                    else:
                        b.emit("dv", "tensor_tensor", y_sb[:, tt, hq * 256:(hq + 1) * 256],
                               y_sb[:, tt, hq * 256:(hq + 1) * 256], pr.ap[:, 0:256],
                               ALU.add, inc=sDV)
                    b.free(pr, sDV)
                dwfree[sl] = sPE.n

        if debug:
            barrier()
            b.wait("sp", sDV)
            b.emit("sp", "dma_start", out=dbg["d_y"], in_=y_sb.rearrange("p a b -> p (a b)"), inc=sOUT, incn=16)

        if stop == "p3":
            return finish()

        b.mark('p3')
        barrier()
        x5free = [sDV.n, sDV.n]
        ob5free = 0
        for tt in range(NCH):
            sl = tt % 2
            b.wait("sp", sDV, x5free[sl])
            b.emit("sp", "dma_start", out=xb5[sl], in_=x_own[tt * 128:(tt + 1) * 128, :],
                   inc=sX[sl], incn=16)
            b.wait("ac", sDV)
            b.emit("ac", "activation", gt5, y_sb[:, tt, :], AF.Silu, inc=sAC)
            b.wait("dv", sAC)
            b.wait("dv", sOUT, ob5free)
            b.emitw("dv", "tensor_tensor", ob5, s_sb[:, tt, :], gt5, ALU.mult)
            b.wait("dv", sX[sl])
            b.emit("dv", "tensor_tensor", ob5, ob5, xb5[sl], ALU.add, inc=sDV)
            x5free[sl] = sDV.n
            b.wait("sp", sDV)
            b.emit("sp", "dma_start", out=out_d[tt * 128:(tt + 1) * 128, :], in_=ob5,
                   inc=sOUT, incn=16)
            ob5free = sOUT.n
        b.wait("sp", sOUT)
        b.mark('p5')

        with nc.Block() as blk:
            @blk.tensor
            def _(t):
                for f in b.q["pe"]:
                    f(t)

            @blk.vector
            def _(v):
                for f in b.q["dv"]:
                    f(v)

            @blk.scalar
            def _(s):
                for f in b.q["ac"]:
                    f(s)

            @blk.sync
            def _(sy):
                for f in b.q["sp"]:
                    f(sy)

            @blk.gpsimd
            def _(g):
                for f in b.q["gp"]:
                    f(g)

    return nc


def _consts():
    bf = ml_dtypes.bfloat16
    ident = np.eye(128, dtype=np.float32)
    ones = np.ones((128, 128), dtype=np.float32)
    tri = np.tril(np.ones((128, 128), dtype=np.float32), -1)
    maskLn = -tri
    maskUn = -tri.T
    maskP = (np.arange(128)[None, :] >= np.arange(128)[:, None]).astype(np.float32)
    cst = np.concatenate([ident, ones, maskLn, maskUn, maskP], axis=1).astype(bf)
    return cst


def kernel(**inputs):
    f32 = lambda v: np.ascontiguousarray(np.asarray(v, dtype=np.float32))
    x = f32(inputs["L0_attn_hidden_states"])
    wq = f32(inputs["L0_attn_Wq"]); wk = f32(inputs["L0_attn_Wk"])
    wv = f32(inputs["L0_attn_Wv"]); wb = f32(inputs["L0_attn_Wb"])
    gam = f32(inputs["L0_attn_o_norm_weight"]).reshape(DH, 1)
    wo = f32(inputs["L0_attn_Wo"])
    gate_w = f32(inputs["L0_ffn_gate_w"])
    w_gate = f32(inputs["L0_ffn_w_gate"]); w_up = f32(inputs["L0_ffn_w_up"])
    w_down = f32(inputs["L0_ffn_w_down"])
    sw_gate = f32(inputs["L0_ffn_sw_gate"]); sw_up = f32(inputs["L0_ffn_sw_up"])
    sw_down = f32(inputs["L0_ffn_sw_down"])
    cst = _consts()
    fconst = np.broadcast_to(np.array(
        [1e-5, -0.5, 1e-24, -0.5 * float(np.log(128.0)), 1.0 / 128.0, 0.0, 1.0, 0.0],
        dtype=np.float32), (128, 8)).copy()
    zeros = np.zeros((TOK, D), dtype=np.float32)

    nc = build()
    in_maps = []
    for c in range(8):
        bb, hh = c // 2, c % 2
        in_maps.append(dict(
            x_own=x[bb, hh * TOK:(hh + 1) * TOK],
            x_pre=(zeros if hh == 0 else x[bb, 0:TOK]),
            wq=wq, wk=wk, wv=wv, wb=wb, gam=gam, wo=wo,
            gate_w=gate_w, w_gate=w_gate, w_up=w_up, w_down=w_down,
            sw_gate=sw_gate, sw_up=sw_up, sw_down=sw_down,
            cst=cst, fconst=fconst,
        ))
    res = run_bass_kernel_spmd(nc, in_maps, core_ids=list(range(8)))
    out = np.empty((4, 2048, D), dtype=np.float32)
    for c in range(8):
        bb, hh = c // 2, c % 2
        out[bb, hh * TOK:(hh + 1) * TOK] = res.results[c]["out"]
    return out
